# version 24
# baseline (speedup 1.0000x reference)
import numpy as np
import ml_dtypes
from contextlib import ExitStack

import concourse.bass as bass
import concourse.mybir as mybir
from concourse.bass_utils import run_bass_kernel_spmd

AF = mybir.ActivationFunctionType
ALU = mybir.AluOpType
AX = mybir.AxisListType
F32 = mybir.dt.float32
BF16 = mybir.dt.bfloat16

T = 8192
D = 1024
DFF = 4096
NE = 8
NT = T // 128
NQG = T // 256
NKT = T // 128
NTG = T // 512
NP = NE + 1
DC = D // 128
FC = DFF // 128

GELU_A = 0.044715
GELU_SQA = float(np.sqrt(0.044715))
GELU_2C = float(2.0 * np.sqrt(2.0 / np.pi))
LN_EPS = 1e-5
SIM_CAP = 1e8

class _Chain:

    def __init__(self, sem):
        self.sem = sem
        self.n = 0
        self.user_tot = {}
        self.handle = None

    def bind(self, eng):
        return _SerEngine(eng, self)


class _PendingIns:
    def __init__(self, ins, chain):
        self._ins = ins
        self._ch = chain

    def then_inc(self, sem, val=1):
        ch = self._ch
        if ch.pending is self:
            tot = ch.user_tot.get(sem.num if hasattr(sem, "num") else id(sem), 0) + val
        key = sem.num if hasattr(sem, "num") else id(sem)
        ch.user_tot[key] = ch.user_tot.get(key, 0) + val
        self._ins.then_inc(sem, val)
        ch.handle = (sem, ch.user_tot[key])
        ch.pending = None
        return self._ins


class _SerEngine:
    _PASS = {"wait_ge", "nop", "drain", "engine_nop"}
    _NOINC = {"dma_start", "dma_start_transpose"}

    def __init__(self, eng, chain):
        self._eng = eng
        self._ch = chain
        if not hasattr(chain, "pending"):
            chain.pending = None

    def _flush(self):
        ch = self._ch
        if ch.pending is not None:
            ch.n += 1
            ch.pending._ins.then_inc(ch.sem, 1)
            ch.handle = (ch.sem, ch.n)
            ch.pending = None

    def __getattr__(self, name):
        f = getattr(self._eng, name)
        if not callable(f) or name.startswith("_") or name in self._PASS:
            return f
        ch = self._ch
        eng = self._eng
        if name in self._NOINC:
            def g(*a, **k):
                self._flush()
                if ch.handle is not None:
                    eng.wait_ge(ch.handle[0], ch.handle[1])
                return f(*a, **k)
            return g

        def g(*a, **k):
            self._flush()
            if ch.handle is not None:
                eng.wait_ge(ch.handle[0], ch.handle[1])
            ins = f(*a, **k)
            p = _PendingIns(ins, ch)
            ch.pending = p
            return p
        return g


_DEBUG = False
_PHASES = "PARF"
_NPASS = NP
_FFN_LEVEL = "full"
_POOL_OFF = False


def build_nc():
    nc = bass.Bass("TRN2", target_bir_lowering=False,
                   dynamic_dma_scratch_size=16384)

    dbg_kind = "ExternalOutput" if _DEBUG else "Internal"

    x_h = nc.dram_tensor("x", [T, D], F32, kind="ExternalInput").ap()
    w1t_sh_h = nc.dram_tensor("w1t_sh", [D, DFF], F32, kind="ExternalInput").ap()
    w2t_sh_h = nc.dram_tensor("w2t_sh", [DFF, D], F32, kind="ExternalInput").ap()
    b1_sh_h = nc.dram_tensor("b1_sh", [128, FC], F32, kind="ExternalInput").ap()
    b2_sh_h = nc.dram_tensor("b2_sh", [2, D // 2], F32,
                             kind="ExternalInput").ap()
    w1t_sp_h = nc.dram_tensor("w1t_sp", [NE, D, DFF], F32, kind="ExternalInput").ap()
    w2t_sp_h = nc.dram_tensor("w2t_sp", [NE, DFF, D], F32, kind="ExternalInput").ap()
    b1_sp_h = nc.dram_tensor("b1_sp", [NE, 128, FC], F32, kind="ExternalInput").ap()
    b2_sp_h = nc.dram_tensor("b2_sp", [NE, 2, D // 2], F32,
                             kind="ExternalInput").ap()
    rwt_h = nc.dram_tensor("rwt", [DC, 128, NE], F32, kind="ExternalInput").ap()
    rb_h = nc.dram_tensor("rb", [1, NE], F32, kind="ExternalInput").ap()

    out_h = nc.dram_tensor("out", [T, D], F32, kind="ExternalOutput").ap()

    v_hbm = nc.dram_tensor("v_hbm", [T, D], BF16, kind=dbg_kind).ap()
    r_hbm = nc.dram_tensor("r_hbm", [T, 1], F32, kind=dbg_kind).ap()
    x1_hbm = nc.dram_tensor("x1_hbm", [T, D], F32, kind=dbg_kind).ap()
    g_hbm = nc.dram_tensor("g_hbm", [T, D], BF16, kind=dbg_kind).ap()
    gt_hbm = nc.dram_tensor("gt_hbm", [D, T], BF16, kind=dbg_kind).ap()
    acc_hbm = nc.dram_tensor("acc_hbm", [T, D], F32, kind=dbg_kind).ap()
    comb_hbm = (nc.dram_tensor("comb_hbm", [128, NT * NE], F32, kind=dbg_kind).ap()
                if _DEBUG else None)
    fdbg_hbm = (nc.dram_tensor("fdbg_hbm", [128, NT * 6], F32, kind=dbg_kind).ap()
                if _DEBUG else None)

    with ExitStack() as outer:
        EO = outer.enter_context
        comb_sb = EO(nc.sbuf_tensor("comb_sb", [128, NT * NE], F32))
        epsb = EO(nc.sbuf_tensor("epsb", [128, 1], F32))

        sems = {}
        for name in [
            "c_eps",
            "p_x0", "p_x1", "p_stat", "p_act", "p_ln", "p_v0", "p_v1",
            "p_r0", "p_r1", "p_lnT",
            "a_so", "a_sim", "a_pt", "a_pv", "a_vr", "a_v0", "a_v1",
            "a_v2", "a_v3", "a_rc", "a_rq", "a_rqc", "a_rql0", "a_rql1",
            "a_ones", "a_fin", "a_f1", "a_x00", "a_x01", "a_x10", "a_x11",
            "a_fdve", "a_fact", "a_actself", "a_fx1", "a_fg",
            "a_x1w0", "a_x1w1", "a_gw0", "a_gw1",
            "r_gt", "r_gtw", "r_rwt", "r_rwtb", "r_rep", "r_lg", "r_dve",
            "r_act", "r_comb", "r_combw",
            "f_w1a", "f_w1b", "f_w2a", "f_w2b", "f_bias1", "f_bias2", "f_b2r",
            "f_g", "f_acc", "f_a1", "f_a2", "f_b",
            "f_mm1", "f_mm2", "f_xse", "f_xso", "f_u", "f_t", "f_sg", "f_h",
            "f_accd", "f_accst",
        ]:
            sems[name] = EO(nc.semaphore(name))
        for name in ["chd", "cha", "chg"]:
            sems[name] = EO(nc.semaphore(name))
        sems["_chains"] = {"d": _Chain(sems["chd"]), "a": _Chain(sems["cha"]),
                          "g": _Chain(sems["chg"])}

        with ExitStack() as ln_scope:
            lnT = ln_scope.enter_context(nc.sbuf_tensor("lnT", [128, DC, T], BF16))
            with ExitStack() as pctx:
                _emit_prologue(nc, sems, pctx, x_h, v_hbm, r_hbm, lnT, epsb)
            if "A" in _PHASES:
                with ExitStack() as actx:
                    _emit_attention(nc, sems, actx, x_h, v_hbm, r_hbm, x1_hbm,
                                    g_hbm, lnT, epsb, fdbg_hbm)
        if "R" in _PHASES:
            with ExitStack() as rctx:
                _emit_relayout_router(nc, sems, rctx, g_hbm, gt_hbm, rwt_h, rb_h,
                                      comb_sb, comb_hbm)
        if "F" in _PHASES:
            with ExitStack() as fctx:
                _emit_ffn(nc, sems, fctx, gt_hbm, x1_hbm, acc_hbm, out_h, comb_sb,
                          w1t_sh_h, w2t_sh_h, b1_sh_h, b2_sh_h,
                          w1t_sp_h, w2t_sp_h, b1_sp_h, b2_sp_h)

    return nc


def _emit_prologue(nc, s, ctx, x_h, v_hbm, r_hbm, lnT, epsb):
    E = ctx.enter_context
    xt = E(nc.sbuf_tensor("p_xt", [128, 2, D], F32))
    lnb = E(nc.sbuf_tensor("p_ln", [128, 2, D], F32))
    sq = E(nc.sbuf_tensor("p_sq", [128, D], F32))
    col = E(nc.sbuf_tensor("p_col", [128, 2, 8], F32))
    with nc.Block("prologue") as block:

        @block.sync
        def _(sync):
            for i in range(NT):
                p = i % 2
                if i >= 2:
                    sync.wait_ge(s["p_ln"], i - 1)
                sync.dma_start(
                    out=xt[:, p, :],
                    in_=x_h[i * 128:(i + 1) * 128, :],
                ).then_inc(s["p_x0" if p == 0 else "p_x1"], 16)
            sync.wait_ge(s["p_v0"], 16 * (NT // 2))
            sync.wait_ge(s["p_v1"], 16 * (NT // 2))
            for c in range(DC):
                sync.dma_start_transpose(
                    out=lnT[:, c, :],
                    in_=v_hbm[:, c * 128:(c + 1) * 128],
                ).then_inc(s["p_lnT"], 16)

        @block.vector
        def _(vector):
            vector = s["_chains"]["d"].bind(vector)
            vector.memset(epsb[:, :], LN_EPS).then_inc(s["c_eps"], 1)
            for i in range(NT):
                p = i % 2
                vector.wait_ge(s["p_x0" if p == 0 else "p_x1"],
                               16 * (i // 2 + 1))
                vector.reduce_sum(col[:, p, 0:1], xt[:, p, :], axis=AX.X,
                                  negate=True).then_inc(s["p_stat"], 1)
                vector.scalar_tensor_tensor(
                    sq[:, :], xt[:, p, :], 1.0, xt[:, p, :],
                    op0=ALU.bypass, op1=ALU.mult,
                    accum_out=col[:, p, 1:2]).then_inc(s["p_stat"], 1)
                vector.tensor_scalar(col[:, p, 2:3], col[:, p, 0:1], -1.0 / D,
                                     None, op0=ALU.mult)
                vector.tensor_scalar(col[:, p, 3:4], col[:, p, 1:2], 1.0 / D,
                                     None, op0=ALU.mult)
                vector.scalar_tensor_tensor(
                    col[:, p, 3:4], col[:, p, 2:3], col[:, p, 2:3],
                    col[:, p, 3:4], op0=ALU.mult, op1=ALU.subtract)
                vector.tensor_copy(sq[:, 0:1],
                                   col[:, p, 3:4]).then_inc(s["p_stat"], 1)
                vector.wait_ge(s["p_act"], 4 * i + 2)
                vector.tensor_copy(sq[:, 1:2], col[:, p, 5:6])
                if i >= 2:
                    vector.wait_ge(s["p_v0" if p == 0 else "p_v1"],
                                   16 * (i // 2))
                vector.tensor_scalar(
                    lnb[:, p, :], xt[:, p, :],
                    col[:, p, 2:3], col[:, p, 5:6],
                    op0=ALU.subtract, op1=ALU.mult).then_inc(s["p_ln"], 1)
                vector.scalar_tensor_tensor(
                    sq[:, :], lnb[:, p, :], 1.0, lnb[:, p, :],
                    op0=ALU.bypass, op1=ALU.mult,
                    accum_out=col[:, p, 6:7])
                vector.tensor_copy(sq[:, 2:3],
                                   col[:, p, 6:7]).then_inc(s["p_stat"], 1)

        @block.scalar
        def _(scalar):
            scalar = s["_chains"]["a"].bind(scalar)
            scalar.wait_ge(s["c_eps"], 1)
            for i in range(NT):
                p = i % 2
                scalar.wait_ge(s["p_stat"], 4 * i + 3)
                scalar.activation(col[:, p, 4:5], col[:, p, 3:4], AF.Ln,
                                  bias=epsb[:, 0:1], scale=-1.0).then_inc(s["p_act"], 1)
                scalar.wait_ge(s["p_act"], 4 * i + 1)
                scalar.activation(col[:, p, 5:6], col[:, p, 4:5], AF.Exp,
                                  scale=-0.5).then_inc(s["p_act"], 1)
                scalar.wait_ge(s["p_stat"], 4 * i + 4)
                scalar.activation(col[:, p, 4:5], col[:, p, 6:7],
                                  AF.Ln).then_inc(s["p_act"], 1)
                scalar.wait_ge(s["p_act"], 4 * i + 3)
                if i >= 2:
                    scalar.wait_ge(s["p_r0" if p == 0 else "p_r1"],
                                   16 * (i // 2))
                scalar.activation(col[:, p, 7:8], col[:, p, 4:5], AF.Exp,
                                  scale=-0.5).then_inc(s["p_act"], 1)
                scalar.dma_start(
                    out=r_hbm[i * 128:(i + 1) * 128, 0:1],
                    in_=col[:, p, 7:8],
                ).then_inc(s["p_r0" if p == 0 else "p_r1"], 16)

        @block.gpsimd
        def _(gpsimd):
            gpsimd = s["_chains"]["g"].bind(gpsimd)
            for i in range(NT):
                p = i % 2
                gpsimd.wait_ge(s["p_ln"], i + 1)
                if i >= 2:
                    gpsimd.wait_ge(s["p_v0" if p == 0 else "p_v1"],
                                   16 * (i // 2))
                gpsimd.dma_start(
                    out=v_hbm[i * 128:(i + 1) * 128, :],
                    in_=lnb[:, p, :],
                ).then_inc(s["p_v0" if p == 0 else "p_v1"], 16)


def _emit_attention(nc, s, ctx, x_h, v_hbm, r_hbm, x1_hbm, g_hbm, lnT, epsb,
                    fdbg_hbm=None):
    def fdx(i, lane):
        return (i * 6 + lane) if fdbg_hbm is not None else lane
    E = ctx.enter_context
    r_q = E(nc.sbuf_tensor("a_rqrow", [1, 2, 256], F32))
    r_col = E(nc.sbuf_tensor("a_rcol", [128, NKT], F32))
    rq_sb = E(nc.sbuf_tensor("a_rq", [128, 256], F32))
    ones_a = E(nc.sbuf_tensor("a_onesf", [1, 128], F32))
    v_sb = E(nc.sbuf_tensor("a_v", [128, 4, 1025], BF16))
    omin = E(nc.sbuf_tensor("a_omin", [128, 2, 256], F32))
    simb = E(nc.sbuf_tensor("a_sim", [128, 2, 256], F32))
    ptb = E(nc.sbuf_tensor("a_pt", [128, 3, 256], BF16))
    xq = E(nc.sbuf_tensor("a_xq", [128, 2, 2, D], F32))
    t1b = E(nc.sbuf_tensor("a_t1", [128, 2, D], F32))
    ln1f = E(nc.sbuf_tensor("a_ln1f", [128, 2, D], F32))
    x1b = E(nc.sbuf_tensor("a_x1", [128, 2, D], F32))
    gqb = E(nc.sbuf_tensor("a_gq", [128, 2, D], BF16))
    sqf = E(nc.sbuf_tensor("a_sqf", [128, D], F32))
    colf = E(nc.sbuf_tensor("a_colf", [128, 20], F32))
    fdbg = E(nc.sbuf_tensor("a_fdbg", [128, NT * 6 if fdbg_hbm is not None
                                        else 8], F32))
    ps_so = E(nc.psum_tensor("a_ps_so", [128, 2, 512], F32))
    pv0 = E(nc.psum_tensor("a_pv0", [128, 1536], F32))
    pv1 = E(nc.psum_tensor("a_pv1", [128, 1536], F32))
    pvs = [pv0, pv1]

    def glob(qg, kt):
        return qg * NKT + kt

    with nc.Block("attention") as block:

        @block.sync
        def _(sync):
            sync.wait_ge(s["p_r0"], 16 * (NT // 2))
            sync.wait_ge(s["p_r1"], 16 * (NT // 2))
            sync.wait_ge(s["p_v0"], 16 * (NT // 2))
            sync.wait_ge(s["p_v1"], 16 * (NT // 2))
            for kt in range(NKT):
                sync.dma_start(out=r_col[:, kt:kt + 1],
                               in_=r_hbm[kt * 128:(kt + 1) * 128, 0:1]).then_inc(
                    s["a_rc"], 16)
            sync.dma_start(
                out=r_q[:, 0, :],
                in_=r_hbm[0:256, 0:1].rearrange("t o -> o t")).then_inc(
                s["a_rql0"], 16)
            for qg in range(NQG):
                if qg + 1 < NQG:
                    if qg >= 1:
                        sync.wait_ge(s["a_rq"], qg)
                    sync.dma_start(
                        out=r_q[:, (qg + 1) % 2, :],
                        in_=r_hbm[(qg + 1) * 256:(qg + 2) * 256, 0:1].rearrange(
                            "t o -> o t")).then_inc(
                        s["a_rql0" if (qg + 1) % 2 == 0 else "a_rql1"], 16)
                if qg >= 2:
                    sync.wait_ge(s["a_fx1"], 2 * (qg - 1))
                for qc in range(2):
                    i = qg * 2 + qc
                    sync.dma_start(
                        out=xq[:, qg % 2, qc, :],
                        in_=x_h[i * 128:(i + 1) * 128, :],
                    ).then_inc(s["a_x%d%d" % (qg % 2, qc)], 16)
                for kt in range(NKT):
                    L = qg * NKT + kt
                    if L >= 4:
                        sync.wait_ge(s["a_pv"], L - 3)
                    sync.dma_start(
                        out=v_sb[:, L % 4, 0:1024],
                        in_=v_hbm[kt * 128:(kt + 1) * 128, :],
                    ).then_inc(s["a_v%d" % (L % 4)], 16)

        @block.vector
        def _(vector):
            vector = s["_chains"]["d"].bind(vector)
            vector.memset(fdbg[:, :], 0.0)
            vector.memset(ones_a[:, :], 1.0).then_inc(s["a_ones"], 1)
            for b in range(4):
                vector.memset(v_sb[:, b, 1024:1025], 1.0).then_inc(
                    s["a_ones"], 1)
            vector.wait_ge(s["a_rc"], 16 * NKT)
            vector.wait_ge(s["a_rq"], 1)
            vector.tensor_copy(rq_sb[:, :],
                               ps_so[:, 0, 256:512]).then_inc(s["a_rqc"], 1)

            def sched(qg):
                out = {}
                def put(slot, f):
                    out.setdefault(slot, []).append(f)

                if qg >= 1:
                    qf = qg - 1

                    def head():
                        vector.wait_ge(s["a_pv"], glob(qf, NKT - 1) + 1)
                        vector.reciprocal(colf[:, 2:3], pvs[0][:, 1024:1025])
                        vector.reciprocal(colf[:, 3:4], pvs[1][:, 1024:1025])
                        vector.tensor_copy(
                            fdbg[:, fdx(qf * 2, 0):fdx(qf * 2, 0) + 1],
                            colf[:, 2:3])
                    put(0, head)

                    def mk(qc):
                        i = qf * 2 + qc
                        def t1op():
                            vector.scalar_tensor_tensor(
                                t1b[:, qc, :], pvs[qc][:, 0:1024],
                                colf[:, 2 + qc:3 + qc], xq[:, qf % 2, qc, :],
                                op0=ALU.mult, op1=ALU.add).then_inc(
                                s["a_fin"], 1)
                        def ln1op():
                            vector.wait_ge(s["a_f1"], 2 * qf + 1 + qc)
                            vector.tensor_copy(
                                fdbg[:, fdx(i, 2):fdx(i, 2) + 1],
                                colf[:, 8 + qc:9 + qc])
                            vector.tensor_scalar(
                                ln1f[:, qc, :], xq[:, qf % 2, qc, :],
                                colf[:, 4 + qc:5 + qc], colf[:, 8 + qc:9 + qc],
                                op0=ALU.subtract, op1=ALU.mult)
                        def x1op():
                            if i >= 2:
                                vector.wait_ge(s["a_x1w%d" % qc], 16 * qf)
                            vector.tensor_tensor(
                                x1b[:, qc, :], t1b[:, qc, :], ln1f[:, qc, :],
                                op=ALU.add).then_inc(s["a_fx1"], 1)
                        def ln2a():
                            vector.reduce_sum(colf[:, 0:1], x1b[:, qc, :],
                                              axis=AX.X, negate=True)
                        def ln2b():
                            vector.scalar_tensor_tensor(
                                sqf[:, :], x1b[:, qc, :], 1.0, x1b[:, qc, :],
                                op0=ALU.bypass, op1=ALU.mult,
                                accum_out=colf[:, 1:2])
                        def ln2c():
                            vector.tensor_scalar(
                                colf[:, 10 + qc:11 + qc], colf[:, 0:1],
                                -1.0 / D, None, op0=ALU.mult)
                            vector.tensor_scalar(
                                colf[:, 12 + qc:13 + qc], colf[:, 1:2],
                                1.0 / D, None, op0=ALU.mult)
                            vector.scalar_tensor_tensor(
                                colf[:, 12 + qc:13 + qc],
                                colf[:, 10 + qc:11 + qc],
                                colf[:, 10 + qc:11 + qc],
                                colf[:, 12 + qc:13 + qc],
                                op0=ALU.mult, op1=ALU.subtract)
                            vector.tensor_copy(
                                fdbg[:, fdx(i, 3):fdx(i, 3) + 1],
                                colf[:, 12 + qc:13 + qc]).then_inc(
                                s["a_fdve"], 1)
                        def gqop():
                            vector.wait_ge(s["a_fact"], 2 * qf + 1 + qc)
                            vector.tensor_copy(
                                fdbg[:, fdx(i, 4):fdx(i, 4) + 1],
                                colf[:, 14 + qc:15 + qc])
                            if i >= 2:
                                vector.wait_ge(s["a_gw%d" % qc], 16 * qf)
                            vector.tensor_scalar(
                                gqb[:, qc, :], x1b[:, qc, :],
                                colf[:, 10 + qc:11 + qc],
                                colf[:, 14 + qc:15 + qc],
                                op0=ALU.subtract, op1=ALU.mult).then_inc(
                                s["a_fg"], 1)
                        return t1op, ln1op, x1op, ln2a, ln2b, ln2c, gqop
                    t10, l10, x10, a0, b0, c0, g0 = mk(0)
                    t11, l11, x11, a1, b1, c1, g1 = mk(1)
                    put(0, t10); put(1, t11); put(2, l10)
                    put(4, x10); put(6, a0)
                    put(8, b0); put(10, c0)
                    put(12, l11)
                    put(14, x11)
                    put(16, a1); put(18, b1)
                    put(20, c1)
                    put(22, g0)
                    put(24, g1)
                if qg >= NQG:
                    return out
                def mkpre(qc):
                    def f1():
                        if qg >= 1:
                            vector.wait_ge(s["a_actself"], 4 * (qg - 1) + 2)
                        vector.wait_ge(s["a_x%d%d" % (qg % 2, qc)],
                                       16 * (qg // 2 + 1))
                        vector.reduce_sum(colf[:, 0:1], xq[:, qg % 2, qc, :],
                                          axis=AX.X, negate=True)
                    def f2():
                        vector.scalar_tensor_tensor(
                            sqf[:, :], xq[:, qg % 2, qc, :], 1.0,
                            xq[:, qg % 2, qc, :],
                            op0=ALU.bypass, op1=ALU.mult,
                            accum_out=colf[:, 1:2])
                    def f3():
                        vector.tensor_scalar(colf[:, 4 + qc:5 + qc],
                                             colf[:, 0:1], -1.0 / D,
                                             None, op0=ALU.mult)
                        vector.tensor_scalar(colf[:, 6 + qc:7 + qc],
                                             colf[:, 1:2], 1.0 / D,
                                             None, op0=ALU.mult)
                    def f4():
                        vector.scalar_tensor_tensor(
                            colf[:, 6 + qc:7 + qc], colf[:, 4 + qc:5 + qc],
                            colf[:, 4 + qc:5 + qc], colf[:, 6 + qc:7 + qc],
                            op0=ALU.mult, op1=ALU.subtract)
                        vector.tensor_copy(
                            fdbg[:, fdx(qg * 2 + qc, 1):
                                 fdx(qg * 2 + qc, 1) + 1],
                            colf[:, 6 + qc:7 + qc]).then_inc(s["a_fdve"], 1)
                    return f1, f2, f3, f4
                p10, p20, p30, p40 = mkpre(0)
                p11, p21, p31, p41 = mkpre(1)
                put(26, p10); put(28, p20); put(30, p30); put(32, p40)
                put(34, p11); put(36, p21); put(38, p31); put(40, p41)
                return out

            for qg in range(NQG):
                plan = sched(qg)
                vector.tensor_scalar(
                    omin[:, 0, :], rq_sb[:, :],
                    r_col[:, 0:1], SIM_CAP,
                    op0=ALU.mult, op1=ALU.min)
                for kt in range(NKT):
                    g = glob(qg, kt)
                    p = kt % 2
                    if kt + 1 < NKT:
                        vector.tensor_scalar(
                            omin[:, 1 - p, :], rq_sb[:, :],
                            r_col[:, kt + 1:kt + 2], SIM_CAP,
                            op0=ALU.mult, op1=ALU.min)
                    if g >= 2:
                        vector.wait_ge(s["a_pt"], g - 1)
                    vector.wait_ge(s["a_so"], g + 1)
                    vector.tensor_tensor(
                        simb[:, p, :], ps_so[:, p, 0:256], omin[:, p, :],
                        op=ALU.mult).then_inc(s["a_sim"], 1)
                    for f in plan.pop(kt, []):
                        f()
                for slot in sorted(plan):
                    for f in plan[slot]:
                        f()
                if qg + 1 < NQG:
                    vector.wait_ge(s["a_rq"], qg + 2)
                    vector.tensor_copy(rq_sb[:, :],
                                       ps_so[:, 0, 256:512]).then_inc(
                        s["a_rqc"], 1)
            plan = sched(NQG)
            for slot in sorted(plan):
                for f in plan[slot]:
                    f()

        @block.scalar
        def _(scalar):
            scalar = s["_chains"]["a"].bind(scalar)

            def sched_act(qg):
                out = {}
                def put(slot, f):
                    out.setdefault(slot, []).append(f)
                if qg < NQG:
                    def mk1(qc):
                        def a1():
                            scalar.wait_ge(s["a_fdve"], 4 * qg + 1 + qc)
                            scalar.activation(
                                colf[:, 16:17], colf[:, 6 + qc:7 + qc], AF.Ln,
                                bias=epsb[:, 0:1], scale=-1.0).then_inc(
                                s["a_actself"], 1)
                        def a2():
                            scalar.wait_ge(s["a_actself"], 4 * qg + 1 + qc)
                            scalar.activation(
                                colf[:, 8 + qc:9 + qc], colf[:, 16:17],
                                AF.Exp, scale=-0.5).then_inc(s["a_f1"], 1)
                        return a1, a2
                    a10, a20 = mk1(0)
                    a11, a21 = mk1(1)
                    put(33, a10); put(34, a20)
                    put(41, a11); put(42, a21)
                if qg >= 1:
                    qf = qg - 1
                    def mk2(qc):
                        i = qf * 2 + qc
                        def a3():
                            scalar.wait_ge(s["a_fx1"], i + 1)
                            scalar.dma_start(
                                out=x1_hbm[i * 128:(i + 1) * 128, :],
                                in_=x1b[:, qc, :]).then_inc(
                                s["a_x1w%d" % qc], 16)
                        def a4():
                            scalar.wait_ge(s["a_fdve"], 4 * qf + 3 + qc)
                            scalar.activation(
                                colf[:, 16:17], colf[:, 12 + qc:13 + qc],
                                AF.Ln,
                                bias=epsb[:, 0:1], scale=-1.0).then_inc(
                                s["a_actself"], 1)
                        def a5():
                            scalar.wait_ge(s["a_actself"], 4 * qf + 3 + qc)
                            scalar.activation(
                                colf[:, 14 + qc:15 + qc], colf[:, 16:17],
                                AF.Exp, scale=-0.5).then_inc(s["a_fact"], 1)
                        def a6():
                            scalar.wait_ge(s["a_fg"], i + 1)
                            scalar.dma_start(
                                out=g_hbm[i * 128:(i + 1) * 128, :],
                                in_=gqb[:, qc, :]).then_inc(
                                s["a_gw%d" % qc], 16)
                        return a3, a4, a5, a6
                    a30, a40, a50, a60 = mk2(0)
                    a31, a41, a51, a61 = mk2(1)
                    put(5, a30); put(11, a40); put(12, a50)
                    put(15, a31); put(21, a41); put(22, a51)
                    put(23, a60); put(25, a61)
                return out

            for qg in range(NQG):
                plan = sched_act(qg)
                for kt in range(NKT):
                    g = glob(qg, kt)
                    p = kt % 2
                    scalar.wait_ge(s["a_sim"], g + 1)
                    if g >= 3:
                        scalar.wait_ge(s["a_pv"], g - 2)
                    scalar.activation(ptb[:, g % 3, :], simb[:, p, :],
                                      AF.Exp).then_inc(s["a_pt"], 1)
                    for f in plan.pop(kt, []):
                        f()
                for slot in sorted(plan):
                    for f in plan[slot]:
                        f()
            plan = sched_act(NQG)
            for slot in sorted(plan):
                for f in plan[slot]:
                    f()
            if fdbg_hbm is not None:
                scalar.wait_ge(s["a_fg"], NT)
                scalar.wait_ge(s["a_gw0"], 16 * (NT // 2))
                scalar.wait_ge(s["a_gw1"], 16 * (NT // 2))
                scalar.dma_start(out=fdbg_hbm[:, :], in_=fdbg[:, :]).then_inc(
                    s["a_gw0"], 16)

        @block.tensor
        def _(tensor):
            tensor.wait_ge(s["p_lnT"], 16 * DC)
            tensor.wait_ge(s["a_ones"], 5)
            tensor.wait_ge(s["a_rql0"], 16)
            tensor.matmul(
                ps_so[:, 0, 256:512], ones_a[:, :], r_q[0:1, 0, :],
                start=True, stop=True).then_inc(s["a_rq"], 1)
            for qg in range(NQG):
                q0 = qg * 256
                tensor.wait_ge(s["a_rqc"], qg + 1)
                for kt in range(NKT):
                    g = glob(qg, kt)
                    p = kt % 2
                    k0 = kt * 128
                    if g >= 2:
                        tensor.wait_ge(s["a_sim"], g - 1)
                    ins = None
                    for c in range(DC):
                        ins = tensor.matmul(
                            ps_so[:, p, 0:256],
                            lnT[:, c, k0:k0 + 128],
                            lnT[:, c, q0:q0 + 256],
                            start=(c == 0), stop=(c == DC - 1))
                    ins.then_inc(s["a_so"], 1)
                    if g >= 2:
                        _attn_pv(tensor, s, g - 2, v_sb, ptb, pvs)
                if qg + 1 < NQG:
                    tensor.wait_ge(s["a_sim"], glob(qg, NKT - 1) + 1)
                    tensor.wait_ge(
                        s["a_rql0" if (qg + 1) % 2 == 0 else "a_rql1"],
                        16 * ((qg + 1) // 2 + 1))
                    tensor.matmul(
                        ps_so[:, 0, 256:512],
                        ones_a[:, :],
                        r_q[0:1, (qg + 1) % 2, :],
                        start=True, stop=True).then_inc(s["a_rq"], 1)
            for g in (NQG * NKT - 2, NQG * NKT - 1):
                _attn_pv(tensor, s, g, v_sb, ptb, pvs)


def _attn_pv(tensor, s, g, v_sb, ptb, pvs):
    qg, kt = divmod(g, NKT)
    tensor.wait_ge(s["a_pt"], g + 1)
    tensor.wait_ge(s["a_v%d" % (g % 4)], 16 * (g // 4 + 1))
    if qg >= 1 and kt == 0:
        tensor.wait_ge(s["a_fin"], 2 * qg)
    first = (kt == 0)
    last = (kt == NKT - 1)
    ins = None
    for qc in range(2):
        pv = pvs[qc]
        pt_chunk = ptb[:, g % 3, qc * 128:(qc + 1) * 128]
        for (c0, c1) in ((0, 512), (512, 1024), (1024, 1025)):
            ins = tensor.matmul(
                pv[:, c0:c1],
                pt_chunk,
                v_sb[:, g % 4, c0:c1],
                start=first, stop=last)
    ins.then_inc(s["a_pv"], 1)


def _emit_relayout_router(nc, s, ctx, g_hbm, gt_hbm, rwt_h, rb_h, comb_sb,
                          comb_hbm):
    E = ctx.enter_context
    gt_sb = E(nc.sbuf_tensor("r_gt", [128, DC, T], BF16))
    rwt_f = E(nc.sbuf_tensor("r_rwtf", [128, DC, NE], F32))
    rwt_b = E(nc.sbuf_tensor("r_rwtb", [128, DC, NE], BF16))
    ones_f = E(nc.sbuf_tensor("r_ones", [1, 128], F32))
    rb_f = E(nc.sbuf_tensor("r_rb", [1, NE], F32))
    rbrep = E(nc.sbuf_tensor("r_rbrep", [128, NE], F32))
    lg = E(nc.sbuf_tensor("r_lg", [128, 2, NE], F32))
    sexp = E(nc.sbuf_tensor("r_sexp", [128, 2, NE], F32))
    m8 = E(nc.sbuf_tensor("r_m8", [128, 2, 8], F32))
    mask = E(nc.sbuf_tensor("r_mask", [128, NE], F32))
    ccol = E(nc.sbuf_tensor("r_ccol", [128, 4], F32))
    ps_lg = E(nc.psum_tensor("r_pslg", [128, 2, 512], F32))
    ps_rep = E(nc.psum_tensor("r_psrep", [128, 512], F32))

    with nc.Block("router") as block:

        @block.sync
        def _(sync):
            sync.wait_ge(s["a_gw0"], 16 * (NT // 2 + 1) if comb_hbm is not None
                         else 16 * (NT // 2))
            sync.wait_ge(s["a_gw1"], 16 * (NT // 2))
            sync.wait_ge(s["a_x1w0"], 16 * (NT // 2))
            sync.wait_ge(s["a_x1w1"], 16 * (NT // 2))
            for c in range(DC):
                sync.dma_start_transpose(
                    out=gt_sb[:, c, :],
                    in_=g_hbm[:, c * 128:(c + 1) * 128],
                ).then_inc(s["r_gt"], 16)
            sync.wait_ge(s["r_gt"], 16 * DC)
            for c in range(DC):
                sync.dma_start(
                    out=gt_hbm[c * 128:(c + 1) * 128, :],
                    in_=gt_sb[:, c, :],
                ).then_inc(s["r_gtw"], 16)
            sync.dma_start(out=rwt_f[:, :, :],
                           in_=rwt_h.rearrange("c p e -> p c e")).then_inc(
                s["r_rwt"], 16)
            sync.dma_start(out=rb_f[:, :], in_=rb_h[:, :]).then_inc(
                s["r_rwt"], 16)

        @block.vector
        def _(vector):
            vector = s["_chains"]["d"].bind(vector)
            vector.memset(ones_f[:, :], 1.0).then_inc(s["r_rep"], 1)
            vector.wait_ge(s["r_rwt"], 32)
            vector.tensor_copy(rwt_b[:, :, :], rwt_f[:, :, :]).then_inc(
                s["r_rwtb"], 1)
            vector.wait_ge(s["r_rep"], 2)
            vector.tensor_copy(rbrep[:, :], ps_rep[:, 0:NE])
            for i in range(NT):
                p = i % 2
                vector.wait_ge(s["r_lg"], i + 1)
                if i >= 2:
                    vector.wait_ge(s["r_act"], i - 1)
                vector.tensor_tensor(lg[:, p, :], ps_lg[:, p, 0:NE], rbrep[:, :],
                                     op=ALU.add)
                vector.tensor_copy(ccol[:, 3:4], rbrep[:, 0:1])
                vector.reduce_max(ccol[:, 0:1], lg[:, p, :], axis=AX.X,
                                  negate=True)
                vector.tensor_copy(ccol[:, 3:4],
                                   ccol[:, 0:1]).then_inc(s["r_dve"], 1)
                vector.wait_ge(s["r_act"], i + 1)
                vector.tensor_copy(ccol[:, 3:4], rbrep[:, 0:1])
                vector.tensor_copy(ccol[:, 3:4], rbrep[:, 1:2])
                vector.max(out=m8[:, p, :], in_=sexp[:, p, :])
                vector.tensor_copy(ccol[:, 3:4], rbrep[:, 0:1])
                vector.reduce_sum(ccol[:, 1:2], m8[:, p, 0:2], axis=AX.X)
                vector.tensor_scalar(mask[:, :], sexp[:, p, :],
                                     m8[:, p, 1:2], None, op0=ALU.is_ge)
                vector.reciprocal(ccol[:, 2:3], ccol[:, 1:2])
                vector.tensor_copy(ccol[:, 3:4], rbrep[:, 0:1])
                vector.tensor_copy(ccol[:, 3:4], rbrep[:, 1:2])
                vector.scalar_tensor_tensor(
                    comb_sb[:, i * NE:(i + 1) * NE], sexp[:, p, :],
                    ccol[:, 2:3], mask[:, :],
                    op0=ALU.mult, op1=ALU.mult).then_inc(s["r_comb"], 1)

        @block.scalar
        def _(scalar):
            scalar = s["_chains"]["a"].bind(scalar)
            for i in range(NT):
                p = i % 2
                scalar.wait_ge(s["r_dve"], i + 1)
                if i >= 2:
                    scalar.wait_ge(s["r_comb"], i - 1)
                scalar.activation(sexp[:, p, :], lg[:, p, :], AF.Exp,
                                  bias=ccol[:, 0:1]).then_inc(s["r_act"], 1)
            if comb_hbm is not None:
                scalar.wait_ge(s["r_comb"], NT)
                scalar.dma_start(out=comb_hbm[:, :], in_=comb_sb[:, :]).then_inc(
                    s["r_combw"], 16)
                scalar.wait_ge(s["r_combw"], 16)

        @block.tensor
        def _(tensor):
            tensor.wait_ge(s["r_rwt"], 32)
            tensor.wait_ge(s["r_rep"], 1)
            tensor.matmul(ps_rep[:, 0:NE], ones_f[:, :], rb_f[:, :],
                          start=True, stop=True).then_inc(s["r_rep"], 1)
            tensor.wait_ge(s["r_gt"], 16 * DC)
            tensor.wait_ge(s["r_rwtb"], 1)
            for i in range(NT):
                p = i % 2
                if i >= 2:
                    tensor.wait_ge(s["r_dve"], i - 1)
                ins = None
                for c in range(DC):
                    ins = tensor.matmul(
                        ps_lg[:, p, 0:NE],
                        gt_sb[:, c, i * 128:(i + 1) * 128],
                        rwt_b[:, c, :],
                        start=(c == 0), stop=(c == DC - 1))
                ins.then_inc(s["r_lg"], 1)


def _emit_ffn(nc, s, ctx, gt_hbm, x1_hbm, acc_hbm, out_h, comb_sb,
              w1t_sh_h, w2t_sh_h, b1_sh_h, b2_sh_h,
              w1t_sp_h, w2t_sp_h, b1_sp_h, b2_sp_h):
    E = ctx.enter_context
    w1 = E(nc.sbuf_tensor("f_w1", [128, DC, DFF], BF16))
    w2 = E(nc.sbuf_tensor("f_w2", [128, FC, D], BF16))
    h_sb = E(nc.sbuf_tensor("f_h", [128, FC, 512], BF16))
    g_sb = E(nc.sbuf_tensor("f_g", [128, DC, 512], BF16))
    acc = E(nc.sbuf_tensor("f_acc", [128, 1, D], F32))
    xs = E(nc.sbuf_tensor("f_xs", [128, 3, 1024], F32))
    ub = E(nc.sbuf_tensor("f_ub", [128, 1024], F32))
    tb = E(nc.sbuf_tensor("f_tb", [128, 2, 1024], F32))
    sg = E(nc.sbuf_tensor("f_sg", [128, 2, 1024], F32))
    b1c = E(nc.sbuf_tensor("f_b1", [128, FC], F32))
    b2b = E(nc.sbuf_tensor("f_b2b", [33, D // 2], BF16))
    onesb = E(nc.sbuf_tensor("f_ones", [33, 128], BF16))
    ph = E(nc.psum_tensor("f_ph", [128, 4, 512], F32))
    py0 = E(nc.psum_tensor("f_py0", [128, D], F32))
    py1 = E(nc.psum_tensor("f_py1", [128, D], F32))
    pys = [py0, py1]

    def fcg(p_, tg, fc):
        return (p_ * NTG + tg) * FC + fc

    def tcg(p_, tg, tc):
        return (p_ * NTG + tg) * 4 + tc

    def tgg(p_, tg):
        return p_ * NTG + tg

    NPA = _NPASS
    LVL = {"dma": 0, "mm1": 1, "gelu": 2, "mm2": 3, "full": 4}[_FFN_LEVEL]

    with nc.Block("ffn") as block:

        @block.gpsimd
        def _(gpsimd):
            gpsimd = s["_chains"]["g"].bind(gpsimd)
            gpsimd.wait_ge(s["r_gtw"], 16 * DC)
            gpsimd.wait_ge(s["a_x1w0"], 16 * (NT // 2))
            gpsimd.wait_ge(s["a_x1w1"], 16 * (NT // 2))
            gpsimd.wait_ge(s["a_gw0"], 16 * (NT // 2))
            gpsimd.wait_ge(s["a_gw1"], 16 * (NT // 2))
            for p_ in range(NPA):
                w1_src = w1t_sh_h if p_ == 0 else w1t_sp_h[p_ - 1]
                w2_src = w2t_sh_h if p_ == 0 else w2t_sp_h[p_ - 1]
                b1_src = b1_sh_h if p_ == 0 else b1_sp_h[p_ - 1]
                b2_src = b2_sh_h if p_ == 0 else b2_sp_h[p_ - 1]
                if p_ > 0 and LVL >= 1:
                    gpsimd.wait_ge(s["f_mm1"], (NTG * p_ - 1) * FC + 16)
                for hh in range(2):
                    gpsimd.dma_start(
                        out=w1[:, hh * 4:(hh + 1) * 4, 0:2048],
                        in_=w1_src[hh * 512:(hh + 1) * 512, 0:2048].rearrange(
                            "(c p) f -> p c f", p=128),
                    ).then_inc(s["f_w1a"], 16)
                if p_ > 0 and LVL >= 1:
                    gpsimd.wait_ge(s["f_mm1"], NTG * p_ * FC)
                for hh in range(2):
                    gpsimd.dma_start(
                        out=w1[:, hh * 4:(hh + 1) * 4, 2048:4096],
                        in_=w1_src[hh * 512:(hh + 1) * 512, 2048:4096].rearrange(
                            "(c p) f -> p c f", p=128),
                    ).then_inc(s["f_w1b"], 16)
                if p_ > 0 and LVL >= 3:
                    gpsimd.wait_ge(s["f_mm2"], 4 * NTG * p_)
                for hh in range(4):
                    gpsimd.dma_start(
                        out=w2[:, hh * 4:(hh + 1) * 4, :],
                        in_=w2_src[hh * 512:(hh + 1) * 512, :].rearrange(
                            "(c p) d -> p c d", p=128),
                    ).then_inc(s["f_w2a"], 16)
                for hh in range(4):
                    gpsimd.dma_start(
                        out=w2[:, 16 + hh * 4:16 + (hh + 1) * 4, :],
                        in_=w2_src[2048 + hh * 512:2048 + (hh + 1) * 512, :].rearrange(
                            "(c p) d -> p c d", p=128),
                    ).then_inc(s["f_w2b"], 16)
                gpsimd.dma_start(out=b1c[:, :], in_=b1_src[:, :]).then_inc(
                    s["f_bias1"], 16)
                gpsimd.dma_start(out=b2b[0:1, :],
                                 in_=b2_src[0:1, :]).then_inc(s["f_bias2"], 16)
                gpsimd.dma_start(out=b2b[32:33, :],
                                 in_=b2_src[1:2, :]).then_inc(s["f_bias2"], 16)

        @block.sync
        def _(sync):
            sync.wait_ge(s["r_gtw"], 16 * DC)
            sync.wait_ge(s["a_x1w0"], 16 * (NT // 2))
            sync.wait_ge(s["a_x1w1"], 16 * (NT // 2))
            for p_ in range(NPA):
                for tg in range(NTG):
                    k0 = tgg(p_, tg)
                    if LVL >= 1:
                        if k0 >= 1:
                            sync.wait_ge(s["f_mm1"], FC * k0)
                        sync.dma_start(
                            out=g_sb[:, :, :],
                            in_=gt_hbm[:, tg * 512:(tg + 1) * 512].rearrange(
                                "(c p) t -> p c t", p=128),
                        ).then_inc(s["f_g"], 16)
                    if LVL >= 4:
                        srcx = x1_hbm if p_ == 0 else acc_hbm
                        for P in range(4):
                            sync.wait_ge(s["f_accst"], 16 * (4 * k0 + P))
                            r0 = tg * 512 + P * 128
                            sync.dma_start(
                                out=acc[:, 0, :],
                                in_=srcx[r0:r0 + 128, :],
                            ).then_inc(s["f_acc"], 16)
            if LVL >= 4:
                sync.wait_ge(s["f_accst"], 64 * NPA * NTG)
            elif LVL >= 2:
                if _POOL_OFF:
                    sync.wait_ge(s["f_sg"], NPA * NTG * FC)
                else:
                    sync.wait_ge(s["f_h"], NPA * NTG * FC)
            elif LVL >= 1:
                sync.wait_ge(s["f_mm1"], FC * NPA * NTG)
            else:
                sync.wait_ge(s["f_w2b"], 16 * NPA)

        @block.vector
        def _(vector):
            vector = s["_chains"]["d"].bind(vector)
            vector.memset(onesb[:, :], 1.0).then_inc(s["f_b2r"], 1)
            for p_ in range(NPA):
                if LVL < 2:
                    continue
                vector.wait_ge(s["f_bias1"], 16 * (p_ + 1))
                vector.engine_nop().then_inc(s["f_b2r"], 1)
                for tg in range(NTG):
                    if LVL >= 2:
                        NQ = FC // 2
                        for q in range(NQ):
                            kq = (p_ * NTG + tg) * NQ + q
                            fc = 2 * q
                            k = fcg(p_, tg, fc)
                            vector.wait_ge(s["f_mm1"], k + 1)
                            if kq >= 3:
                                vector.wait_ge(s["f_u"], kq - 2)
                            vector.tensor_scalar(
                                xs[:, kq % 3, 0:512], ph[:, fc % 4, :],
                                b1c[:, fc:fc + 1], None,
                                op0=ALU.add).then_inc(s["f_xse"], 1)
                            if q >= 2:
                                vector.wait_ge(s["f_sg"], kq - 1)
                                vector.tensor_tensor(
                                    h_sb[:, 2 * (q - 2):2 * (q - 1), :],
                                    xs[:, (kq - 2) % 3, :],
                                    sg[:, (q - 2) % 2, :],
                                    op=ALU.mult).then_inc(s["f_h"], 1)
                            vector.wait_ge(s["f_u"], kq + 1)
                            if kq >= 2:
                                vector.wait_ge(s["f_sg"], kq - 1)
                            vector.scalar_tensor_tensor(
                                tb[:, kq % 2, :], ub[:, :], 1.0,
                                xs[:, kq % 3, :],
                                op0=ALU.add, op1=ALU.mult).then_inc(s["f_t"], 1)
                        for qh in (NQ - 2, NQ - 1):
                            kq = (p_ * NTG + tg) * NQ + qh
                            vector.wait_ge(s["f_sg"], kq + 1)
                            vector.tensor_tensor(
                                h_sb[:, 2 * qh:2 * qh + 2, :],
                                xs[:, kq % 3, :],
                                sg[:, qh % 2, :],
                                op=ALU.mult).then_inc(s["f_h"], 1)
                    if LVL >= 4:
                        for tc in range(4):
                            kt = tcg(p_, tg, tc)
                            vector.wait_ge(s["f_acc"],
                                           16 * (4 * tgg(p_, tg) + tc + 1))
                            vector.wait_ge(s["f_mm2"], kt + 1)
                            if p_ == 0:
                                we = 1.0
                            else:
                                itile = tg * 4 + tc
                                we = comb_sb[:, itile * NE + (p_ - 1):
                                             itile * NE + p_]
                            vector.scalar_tensor_tensor(
                                acc[:, 0, :], pys[tc % 2][:, :], we,
                                acc[:, 0, :],
                                op0=ALU.mult, op1=ALU.add).then_inc(s["f_accd"], 1)

        @block.scalar
        def _(scalar):
            scalar = s["_chains"]["a"].bind(scalar)
            for p_ in range(NPA):
                if LVL < 2:
                    continue
                for tg in range(NTG):
                    if LVL >= 4:
                        tp_, tgp = (p_, tg - 1) if tg > 0 else (p_ - 1, NTG - 1)
                        if tp_ >= 0:
                            kprev = tgg(tp_, tgp)
                            dstp = out_h if tp_ == NPA - 1 else acc_hbm
                            for P in range(4):
                                scalar.wait_ge(s["f_accd"], 4 * kprev + P + 1)
                                r0 = tgp * 512 + P * 128
                                scalar.dma_start(
                                    out=dstp[r0:r0 + 128, :],
                                    in_=acc[:, 0, :],
                                ).then_inc(s["f_accst"], 16)
                    NQ = FC // 2
                    if tg == 0:
                        scalar.wait_ge(s["f_bias1"], 16 * (p_ + 1))
                    for q in range(NQ):
                        fc = 2 * q + 1
                        k = fcg(p_, tg, fc)
                        kq = (p_ * NTG + tg) * NQ + q
                        scalar.wait_ge(s["f_mm1"], k + 1)
                        if kq >= 3:
                            scalar.wait_ge(s["f_t"], kq - 2)
                            scalar.wait_ge(s["f_h"], kq - 2)
                        scalar.activation(
                            xs[:, kq % 3, 512:1024],
                            ph[:, fc % 4, :], AF.Identity,
                            bias=b1c[:, fc:fc + 1]).then_inc(s["f_xso"], 1)
                        scalar.wait_ge(s["f_xse"], kq + 1)
                        if kq >= 1:
                            scalar.wait_ge(s["f_t"], kq)
                        scalar.activation(
                            ub[:, :], xs[:, kq % 3, :], AF.Square,
                            scale=GELU_SQA).then_inc(s["f_u"], 1)
                        if q >= 1:
                            scalar.wait_ge(s["f_t"], kq)
                            if kq >= 2:
                                scalar.wait_ge(s["f_h"], kq - 2)
                            scalar.activation(
                                sg[:, (q - 1) % 2, :],
                                tb[:, (kq - 1) % 2, :], AF.Sigmoid,
                                scale=GELU_2C).then_inc(s["f_sg"], 1)
                    if LVL >= 2:
                        kq = (p_ * NTG + tg) * (FC // 2) + FC // 2 - 1
                        scalar.wait_ge(s["f_t"], kq + 1)
                        if kq >= 2:
                            scalar.wait_ge(s["f_h"], kq - 1)
                        scalar.activation(
                            sg[:, (FC // 2 - 1) % 2, :], tb[:, kq % 2, :],
                            AF.Sigmoid,
                            scale=GELU_2C).then_inc(s["f_sg"], 1)
                if LVL >= 4 and p_ == NPA - 1 and tg == NTG - 1:
                    kprev = tgg(p_, tg)
                    for P in range(4):
                        scalar.wait_ge(s["f_accd"], 4 * kprev + P + 1)
                        r0 = tg * 512 + P * 128
                        scalar.dma_start(
                            out=out_h[r0:r0 + 128, :],
                            in_=acc[:, 0, :],
                        ).then_inc(s["f_accst"], 16)

        @block.tensor
        def _(tensor):
            for p_ in range(NPA):
                if LVL < 1:
                    continue
                for tg in range(NTG):
                    k0 = tgg(p_, tg)
                    for fc in range(FC):
                        pp = fc % 4
                        k = fcg(p_, tg, fc)
                        w1sem = s["f_w1a"] if fc < 16 else s["f_w1b"]
                        tensor.wait_ge(w1sem, 32 * (p_ + 1))
                        tensor.wait_ge(s["f_g"], 16 * (k0 + 1))
                        if k >= 4 and LVL >= 2:
                            kf = k - 4
                            if kf % 2 == 0:
                                tensor.wait_ge(s["f_xse"], kf // 2 + 1)
                            else:
                                tensor.wait_ge(s["f_xso"], kf // 2 + 1)
                        ins = None
                        for c in range(DC):
                            ins = tensor.matmul(
                                ph[:, pp, :],
                                w1[:, c, fc * 128:(fc + 1) * 128],
                                g_sb[:, c, :],
                                start=(c == 0), stop=(c == DC - 1))
                        ins.then_inc(s["f_mm1"], 1)

                    if LVL < 3:
                        continue
                    tensor.wait_ge(s["f_b2r"], 1)
                    tensor.wait_ge(s["f_bias2"], 32 * (p_ + 1))
                    for tc in range(4):
                        kt = tcg(p_, tg, tc)
                        py = pys[tc % 2]
                        if kt >= 2 and LVL >= 4:
                            tensor.wait_ge(s["f_accd"], kt - 1)
                        for fc in range(FC):
                            w2sem = s["f_w2a"] if fc < 16 else s["f_w2b"]
                            tensor.wait_ge(w2sem, 64 * (p_ + 1))
                            if tc == 0:
                                tensor.wait_ge(
                                    s["f_h"],
                                    (p_ * NTG + tg) * (FC // 2) + fc // 2 + 1)
                            for dmh in range(2):
                                tensor.matmul(
                                    py[:, dmh * 512:(dmh + 1) * 512],
                                    h_sb[:, fc, tc * 128:(tc + 1) * 128],
                                    w2[:, fc, dmh * 512:(dmh + 1) * 512],
                                    start=(fc == 0), stop=False)
                        ins = None
                        for dmh in range(2):
                            ins = tensor.matmul(
                                py[:, dmh * 512:(dmh + 1) * 512],
                                onesb[32 * dmh:32 * dmh + 1, :],
                                b2b[32 * dmh:32 * dmh + 1, :],
                                start=False, stop=True)
                        ins.then_inc(s["f_mm2"], 1)



def kernel(L0_attn_x, L0_ffn_shared_w1, L0_ffn_shared_b1, L0_ffn_shared_w2,
           L0_ffn_shared_b2, L0_ffn_router_w, L0_ffn_router_b,
           L0_ffn_spec_w1, L0_ffn_spec_b1, L0_ffn_spec_w2, L0_ffn_spec_b2):
    x = np.ascontiguousarray(np.asarray(L0_attn_x, dtype=np.float32))
    B = x.shape[0]

    def c(a):
        return np.ascontiguousarray(np.asarray(a, dtype=np.float32))

    w1t_sh = c(np.asarray(L0_ffn_shared_w1).T)
    w2t_sh = c(np.asarray(L0_ffn_shared_w2).T)
    b1_sh = c(np.asarray(L0_ffn_shared_b1).reshape(FC, 128).T)
    b2_sh = c(np.asarray(L0_ffn_shared_b2).reshape(2, D // 2))
    w1t_sp = c(np.transpose(np.asarray(L0_ffn_spec_w1), (0, 2, 1)))
    w2t_sp = c(np.transpose(np.asarray(L0_ffn_spec_w2), (0, 2, 1)))
    b1_sp = c(np.transpose(np.asarray(L0_ffn_spec_b1).reshape(NE, FC, 128),
                           (0, 2, 1)))
    b2_sp = c(np.asarray(L0_ffn_spec_b2).reshape(NE, 2, D // 2))
    rwt = c(np.asarray(L0_ffn_router_w).T.reshape(DC, 128, NE))
    rb = c(np.asarray(L0_ffn_router_b).reshape(1, NE))

    nc = build_nc()
    in_maps = []
    for b in range(B):
        in_maps.append({
            "x": np.ascontiguousarray(x[b]),
            "w1t_sh": w1t_sh, "w2t_sh": w2t_sh, "b1_sh": b1_sh, "b2_sh": b2_sh,
            "w1t_sp": w1t_sp, "w2t_sp": w2t_sp, "b1_sp": b1_sp, "b2_sp": b2_sp,
            "rwt": rwt, "rb": rb,
        })
    res = run_bass_kernel_spmd(nc, in_maps, core_ids=list(range(B)))
    out = np.stack([res.results[b]["out"] for b in range(B)], axis=0)
    return out


# version 25
# speedup vs baseline: 1.2019x; 1.2019x over previous
import numpy as np
import ml_dtypes
from contextlib import ExitStack

import concourse.bass as bass
import concourse.mybir as mybir
from concourse.bass_utils import run_bass_kernel_spmd

AF = mybir.ActivationFunctionType
ALU = mybir.AluOpType
AX = mybir.AxisListType
F32 = mybir.dt.float32
BF16 = mybir.dt.bfloat16

T = 8192
D = 1024
DFF = 4096
NE = 8
NT = T // 128
NQG = T // 256
NKT = T // 128
NTG = T // 512
NP = NE + 1
DC = D // 128
FC = DFF // 128

GELU_A = 0.044715
GELU_SQA = float(np.sqrt(0.044715))
GELU_2C = float(2.0 * np.sqrt(2.0 / np.pi))
LN_EPS = 1e-5
SIM_CAP = 1e8

class _Chain:

    def __init__(self, sem):
        self.sem = sem
        self.n = 0
        self.user_tot = {}
        self.handle = None

    def bind(self, eng):
        return _SerEngine(eng, self)


class _PendingIns:
    def __init__(self, ins, chain):
        self._ins = ins
        self._ch = chain

    def then_inc(self, sem, val=1):
        ch = self._ch
        if ch.pending is self:
            tot = ch.user_tot.get(sem.num if hasattr(sem, "num") else id(sem), 0) + val
        key = sem.num if hasattr(sem, "num") else id(sem)
        ch.user_tot[key] = ch.user_tot.get(key, 0) + val
        self._ins.then_inc(sem, val)
        ch.handle = (sem, ch.user_tot[key])
        ch.pending = None
        return self._ins


class _SerEngine:
    _PASS = {"wait_ge", "nop", "drain", "engine_nop"}
    _NOINC = {"dma_start", "dma_start_transpose"}

    def __init__(self, eng, chain):
        self._eng = eng
        self._ch = chain
        if not hasattr(chain, "pending"):
            chain.pending = None

    def _flush(self):
        ch = self._ch
        if ch.pending is not None:
            ch.n += 1
            ch.pending._ins.then_inc(ch.sem, 1)
            ch.handle = (ch.sem, ch.n)
            ch.pending = None

    def __getattr__(self, name):
        f = getattr(self._eng, name)
        if not callable(f) or name.startswith("_") or name in self._PASS:
            return f
        ch = self._ch
        eng = self._eng
        if name in self._NOINC:
            def g(*a, **k):
                self._flush()
                if ch.handle is not None:
                    eng.wait_ge(ch.handle[0], ch.handle[1])
                return f(*a, **k)
            return g

        def g(*a, **k):
            self._flush()
            if ch.handle is not None:
                eng.wait_ge(ch.handle[0], ch.handle[1])
            ins = f(*a, **k)
            p = _PendingIns(ins, ch)
            ch.pending = p
            return p
        return g


_DEBUG = False
_PHASES = "PARF"
_NPASS = NP
_FFN_LEVEL = "full"
_POOL_OFF = False


def build_nc():
    nc = bass.Bass("TRN2", target_bir_lowering=False,
                   dynamic_dma_scratch_size=16384)

    dbg_kind = "ExternalOutput" if _DEBUG else "Internal"

    x_h = nc.dram_tensor("x", [T, D], F32, kind="ExternalInput").ap()
    w1t_sh_h = nc.dram_tensor("w1t_sh", [D, DFF], F32, kind="ExternalInput").ap()
    w2t_sh_h = nc.dram_tensor("w2t_sh", [DFF, D], F32, kind="ExternalInput").ap()
    b1_sh_h = nc.dram_tensor("b1_sh", [128, FC], F32, kind="ExternalInput").ap()
    b2_sh_h = nc.dram_tensor("b2_sh", [2, D // 2], F32,
                             kind="ExternalInput").ap()
    w1t_sp_h = nc.dram_tensor("w1t_sp", [NE, D, DFF], F32, kind="ExternalInput").ap()
    w2t_sp_h = nc.dram_tensor("w2t_sp", [NE, DFF, D], F32, kind="ExternalInput").ap()
    b1_sp_h = nc.dram_tensor("b1_sp", [NE, 128, FC], F32, kind="ExternalInput").ap()
    b2_sp_h = nc.dram_tensor("b2_sp", [NE, 2, D // 2], F32,
                             kind="ExternalInput").ap()
    rwt_h = nc.dram_tensor("rwt", [DC, 128, NE], F32, kind="ExternalInput").ap()
    rb_h = nc.dram_tensor("rb", [1, NE], F32, kind="ExternalInput").ap()

    out_h = nc.dram_tensor("out", [T, D], F32, kind="ExternalOutput").ap()

    v_hbm = nc.dram_tensor("v_hbm", [T, D], BF16, kind=dbg_kind).ap()
    r_hbm = nc.dram_tensor("r_hbm", [T, 1], F32, kind=dbg_kind).ap()
    x1_hbm = nc.dram_tensor("x1_hbm", [T, D], F32, kind=dbg_kind).ap()
    g_hbm = nc.dram_tensor("g_hbm", [T, D], BF16, kind=dbg_kind).ap()
    gt_hbm = nc.dram_tensor("gt_hbm", [D, T], BF16, kind=dbg_kind).ap()
    acc_hbm = nc.dram_tensor("acc_hbm", [T, D], F32, kind=dbg_kind).ap()
    comb_hbm = (nc.dram_tensor("comb_hbm", [128, NT * NE], F32, kind=dbg_kind).ap()
                if _DEBUG else None)
    fdbg_hbm = (nc.dram_tensor("fdbg_hbm", [128, NT * 6], F32, kind=dbg_kind).ap()
                if _DEBUG else None)

    with ExitStack() as outer:
        EO = outer.enter_context
        comb_sb = EO(nc.sbuf_tensor("comb_sb", [128, NT * NE], F32))
        epsb = EO(nc.sbuf_tensor("epsb", [128, 1], F32))

        sems = {}
        for name in [
            "c_eps",
            "p_x0", "p_x1", "p_stat", "p_act", "p_ln", "p_v0", "p_v1",
            "p_r0", "p_r1", "p_lnT",
            "a_so", "a_sim", "a_pt", "a_pv", "a_vr", "a_v0", "a_v1",
            "a_v2", "a_v3", "a_rc", "a_rq", "a_rqc", "a_rql0", "a_rql1",
            "a_ones", "a_fin", "a_f1", "a_x00", "a_x01", "a_x10", "a_x11",
            "a_fdve", "a_fact", "a_actself", "a_fx1", "a_fg",
            "a_x1w0", "a_x1w1", "a_gw0", "a_gw1",
            "r_gt", "r_gtw", "r_rwt", "r_rwtb", "r_rep", "r_lg", "r_dve",
            "r_act", "r_comb", "r_combw",
            "f_w1a", "f_w1b", "f_w2a", "f_w2b", "f_bias1", "f_bias2", "f_b2r",
            "f_g", "f_acc", "f_a1", "f_a2", "f_b",
            "f_mm1", "f_mm2", "f_xse", "f_xso", "f_u", "f_t", "f_sg", "f_h",
            "f_accd", "f_accst",
        ]:
            sems[name] = EO(nc.semaphore(name))
        for name in ["chd", "cha", "chg"]:
            sems[name] = EO(nc.semaphore(name))
        sems["_chains"] = {"d": _Chain(sems["chd"]), "a": _Chain(sems["cha"]),
                          "g": _Chain(sems["chg"])}

        with ExitStack() as ln_scope:
            lnT = ln_scope.enter_context(nc.sbuf_tensor("lnT", [128, DC, T], BF16))
            with ExitStack() as pctx:
                _emit_prologue(nc, sems, pctx, x_h, v_hbm, r_hbm, lnT, epsb)
            if "A" in _PHASES:
                with ExitStack() as actx:
                    _emit_attention(nc, sems, actx, x_h, v_hbm, r_hbm, x1_hbm,
                                    g_hbm, lnT, epsb, fdbg_hbm)
        if "R" in _PHASES:
            with ExitStack() as rctx:
                _emit_relayout_router(nc, sems, rctx, g_hbm, gt_hbm, rwt_h, rb_h,
                                      comb_sb, comb_hbm)
        if "F" in _PHASES:
            with ExitStack() as fctx:
                _emit_ffn(nc, sems, fctx, gt_hbm, x1_hbm, acc_hbm, out_h, comb_sb,
                          w1t_sh_h, w2t_sh_h, b1_sh_h, b2_sh_h,
                          w1t_sp_h, w2t_sp_h, b1_sp_h, b2_sp_h)

    return nc


def _emit_prologue(nc, s, ctx, x_h, v_hbm, r_hbm, lnT, epsb):
    E = ctx.enter_context
    xt = E(nc.sbuf_tensor("p_xt", [128, 2, D], F32))
    lnb = E(nc.sbuf_tensor("p_ln", [128, 2, D], F32))
    sq = E(nc.sbuf_tensor("p_sq", [128, D], F32))
    col = E(nc.sbuf_tensor("p_col", [128, 2, 8], F32))
    with nc.Block("prologue") as block:

        @block.sync
        def _(sync):
            for i in range(NT):
                p = i % 2
                if i >= 2:
                    sync.wait_ge(s["p_ln"], i - 1)
                sync.dma_start(
                    out=xt[:, p, :],
                    in_=x_h[i * 128:(i + 1) * 128, :],
                ).then_inc(s["p_x0" if p == 0 else "p_x1"], 16)
            sync.wait_ge(s["p_v0"], 16 * (NT // 2))
            sync.wait_ge(s["p_v1"], 16 * (NT // 2))
            for c in range(DC):
                sync.dma_start_transpose(
                    out=lnT[:, c, :],
                    in_=v_hbm[:, c * 128:(c + 1) * 128],
                ).then_inc(s["p_lnT"], 16)

        @block.vector
        def _(vector):
            vector = s["_chains"]["d"].bind(vector)
            vector.memset(epsb[:, :], LN_EPS).then_inc(s["c_eps"], 1)
            for i in range(NT):
                p = i % 2
                vector.wait_ge(s["p_x0" if p == 0 else "p_x1"],
                               16 * (i // 2 + 1))
                vector.reduce_sum(col[:, p, 0:1], xt[:, p, :], axis=AX.X,
                                  negate=True).then_inc(s["p_stat"], 1)
                vector.scalar_tensor_tensor(
                    sq[:, :], xt[:, p, :], 1.0, xt[:, p, :],
                    op0=ALU.bypass, op1=ALU.mult,
                    accum_out=col[:, p, 1:2]).then_inc(s["p_stat"], 1)
                vector.tensor_scalar(col[:, p, 2:3], col[:, p, 0:1], -1.0 / D,
                                     None, op0=ALU.mult)
                vector.tensor_scalar(col[:, p, 3:4], col[:, p, 1:2], 1.0 / D,
                                     None, op0=ALU.mult)
                vector.scalar_tensor_tensor(
                    col[:, p, 3:4], col[:, p, 2:3], col[:, p, 2:3],
                    col[:, p, 3:4], op0=ALU.mult, op1=ALU.subtract)
                vector.tensor_copy(sq[:, 0:1],
                                   col[:, p, 3:4]).then_inc(s["p_stat"], 1)
                vector.wait_ge(s["p_act"], 4 * i + 2)
                vector.tensor_copy(sq[:, 1:2], col[:, p, 5:6])
                if i >= 2:
                    vector.wait_ge(s["p_v0" if p == 0 else "p_v1"],
                                   16 * (i // 2))
                vector.tensor_scalar(
                    lnb[:, p, :], xt[:, p, :],
                    col[:, p, 2:3], col[:, p, 5:6],
                    op0=ALU.subtract, op1=ALU.mult).then_inc(s["p_ln"], 1)
                vector.scalar_tensor_tensor(
                    sq[:, :], lnb[:, p, :], 1.0, lnb[:, p, :],
                    op0=ALU.bypass, op1=ALU.mult,
                    accum_out=col[:, p, 6:7])
                vector.tensor_copy(sq[:, 2:3],
                                   col[:, p, 6:7]).then_inc(s["p_stat"], 1)

        @block.scalar
        def _(scalar):
            scalar = s["_chains"]["a"].bind(scalar)
            scalar.wait_ge(s["c_eps"], 1)
            for i in range(NT):
                p = i % 2
                scalar.wait_ge(s["p_stat"], 4 * i + 3)
                scalar.activation(col[:, p, 4:5], col[:, p, 3:4], AF.Ln,
                                  bias=epsb[:, 0:1], scale=-1.0).then_inc(s["p_act"], 1)
                scalar.wait_ge(s["p_act"], 4 * i + 1)
                scalar.activation(col[:, p, 5:6], col[:, p, 4:5], AF.Exp,
                                  scale=-0.5).then_inc(s["p_act"], 1)
                scalar.wait_ge(s["p_stat"], 4 * i + 4)
                scalar.activation(col[:, p, 4:5], col[:, p, 6:7],
                                  AF.Ln).then_inc(s["p_act"], 1)
                scalar.wait_ge(s["p_act"], 4 * i + 3)
                if i >= 2:
                    scalar.wait_ge(s["p_r0" if p == 0 else "p_r1"],
                                   16 * (i // 2))
                scalar.activation(col[:, p, 7:8], col[:, p, 4:5], AF.Exp,
                                  scale=-0.5).then_inc(s["p_act"], 1)
                scalar.dma_start(
                    out=r_hbm[i * 128:(i + 1) * 128, 0:1],
                    in_=col[:, p, 7:8],
                ).then_inc(s["p_r0" if p == 0 else "p_r1"], 16)

        @block.gpsimd
        def _(gpsimd):
            gpsimd = s["_chains"]["g"].bind(gpsimd)
            for i in range(NT):
                p = i % 2
                gpsimd.wait_ge(s["p_ln"], i + 1)
                if i >= 2:
                    gpsimd.wait_ge(s["p_v0" if p == 0 else "p_v1"],
                                   16 * (i // 2))
                gpsimd.dma_start(
                    out=v_hbm[i * 128:(i + 1) * 128, :],
                    in_=lnb[:, p, :],
                ).then_inc(s["p_v0" if p == 0 else "p_v1"], 16)


def _emit_attention(nc, s, ctx, x_h, v_hbm, r_hbm, x1_hbm, g_hbm, lnT, epsb,
                    fdbg_hbm=None):
    def fdx(i, lane):
        return (i * 6 + lane) if fdbg_hbm is not None else lane
    E = ctx.enter_context
    r_q = E(nc.sbuf_tensor("a_rqrow", [1, 2, 256], F32))
    r_col = E(nc.sbuf_tensor("a_rcol", [128, NKT], F32))
    rq_sb = E(nc.sbuf_tensor("a_rq", [128, 256], F32))
    ones_a = E(nc.sbuf_tensor("a_onesf", [1, 128], F32))
    v_sb = E(nc.sbuf_tensor("a_v", [128, 4, 1025], BF16))
    omin = E(nc.sbuf_tensor("a_omin", [128, 2, 256], F32))
    simb = E(nc.sbuf_tensor("a_sim", [128, 2, 256], F32))
    ptb = E(nc.sbuf_tensor("a_pt", [128, 3, 256], BF16))
    xq = E(nc.sbuf_tensor("a_xq", [128, 2, 2, D], F32))
    t1b = E(nc.sbuf_tensor("a_t1", [128, 2, D], F32))
    ln1f = E(nc.sbuf_tensor("a_ln1f", [128, 2, D], F32))
    x1b = E(nc.sbuf_tensor("a_x1", [128, 2, D], F32))
    gqb = E(nc.sbuf_tensor("a_gq", [128, 2, D], BF16))
    sqf = E(nc.sbuf_tensor("a_sqf", [128, D], F32))
    colf = E(nc.sbuf_tensor("a_colf", [128, 20], F32))
    fdbg = E(nc.sbuf_tensor("a_fdbg", [128, NT * 6 if fdbg_hbm is not None
                                        else 8], F32))
    ps_so = E(nc.psum_tensor("a_ps_so", [128, 2, 512], F32))
    pv0 = E(nc.psum_tensor("a_pv0", [128, 1536], F32))
    pv1 = E(nc.psum_tensor("a_pv1", [128, 1536], F32))
    pvs = [pv0, pv1]

    def glob(qg, kt):
        return qg * NKT + kt

    with nc.Block("attention") as block:

        @block.sync
        def _(sync):
            sync.wait_ge(s["p_r0"], 16 * (NT // 2))
            sync.wait_ge(s["p_r1"], 16 * (NT // 2))
            sync.wait_ge(s["p_v0"], 16 * (NT // 2))
            sync.wait_ge(s["p_v1"], 16 * (NT // 2))
            for kt in range(NKT):
                sync.dma_start(out=r_col[:, kt:kt + 1],
                               in_=r_hbm[kt * 128:(kt + 1) * 128, 0:1]).then_inc(
                    s["a_rc"], 16)
            sync.dma_start(
                out=r_q[:, 0, :],
                in_=r_hbm[0:256, 0:1].rearrange("t o -> o t")).then_inc(
                s["a_rql0"], 16)
            for qg in range(NQG):
                if qg + 1 < NQG:
                    if qg >= 1:
                        sync.wait_ge(s["a_rq"], qg)
                    sync.dma_start(
                        out=r_q[:, (qg + 1) % 2, :],
                        in_=r_hbm[(qg + 1) * 256:(qg + 2) * 256, 0:1].rearrange(
                            "t o -> o t")).then_inc(
                        s["a_rql0" if (qg + 1) % 2 == 0 else "a_rql1"], 16)
                if qg >= 2:
                    sync.wait_ge(s["a_fx1"], 2 * (qg - 1))
                for qc in range(2):
                    i = qg * 2 + qc
                    sync.dma_start(
                        out=xq[:, qg % 2, qc, :],
                        in_=x_h[i * 128:(i + 1) * 128, :],
                    ).then_inc(s["a_x%d%d" % (qg % 2, qc)], 16)
                for kt in range(NKT):
                    L = qg * NKT + kt
                    if L >= 4:
                        sync.wait_ge(s["a_pv"], L - 3)
                    sync.dma_start(
                        out=v_sb[:, L % 4, 0:1024],
                        in_=v_hbm[kt * 128:(kt + 1) * 128, :],
                    ).then_inc(s["a_v%d" % (L % 4)], 16)

        @block.vector
        def _(vector):
            vector = s["_chains"]["d"].bind(vector)
            vector.memset(fdbg[:, :], 0.0)
            vector.memset(ones_a[:, :], 1.0).then_inc(s["a_ones"], 1)
            for b in range(4):
                vector.memset(v_sb[:, b, 1024:1025], 1.0).then_inc(
                    s["a_ones"], 1)
            vector.wait_ge(s["a_rc"], 16 * NKT)
            vector.wait_ge(s["a_rq"], 1)
            vector.tensor_copy(rq_sb[:, :],
                               ps_so[:, 0, 256:512]).then_inc(s["a_rqc"], 1)

            def sched(qg):
                out = {}
                def put(slot, f):
                    out.setdefault(slot, []).append(f)

                if qg >= 1:
                    qf = qg - 1

                    def head():
                        vector.wait_ge(s["a_pv"], glob(qf, NKT - 1) + 1)
                        vector.reciprocal(colf[:, 2:3], pvs[0][:, 1024:1025])
                        vector.reciprocal(colf[:, 3:4], pvs[1][:, 1024:1025])
                        vector.tensor_copy(
                            fdbg[:, fdx(qf * 2, 0):fdx(qf * 2, 0) + 1],
                            colf[:, 2:3])
                    put(0, head)

                    def mk(qc):
                        i = qf * 2 + qc
                        def t1op():
                            vector.scalar_tensor_tensor(
                                t1b[:, qc, :], pvs[qc][:, 0:1024],
                                colf[:, 2 + qc:3 + qc], xq[:, qf % 2, qc, :],
                                op0=ALU.mult, op1=ALU.add).then_inc(
                                s["a_fin"], 1)
                        def ln1op():
                            vector.wait_ge(s["a_f1"], 2 * qf + 1 + qc)
                            vector.tensor_copy(
                                fdbg[:, fdx(i, 2):fdx(i, 2) + 1],
                                colf[:, 8 + qc:9 + qc])
                            vector.tensor_scalar(
                                ln1f[:, qc, :], xq[:, qf % 2, qc, :],
                                colf[:, 4 + qc:5 + qc], colf[:, 8 + qc:9 + qc],
                                op0=ALU.subtract, op1=ALU.mult)
                        def x1op():
                            if i >= 2:
                                vector.wait_ge(s["a_x1w%d" % qc], 16 * qf)
                            vector.tensor_tensor(
                                x1b[:, qc, :], t1b[:, qc, :], ln1f[:, qc, :],
                                op=ALU.add).then_inc(s["a_fx1"], 1)
                        def ln2a():
                            vector.reduce_sum(colf[:, 0:1], x1b[:, qc, :],
                                              axis=AX.X, negate=True)
                        def ln2b():
                            vector.scalar_tensor_tensor(
                                sqf[:, :], x1b[:, qc, :], 1.0, x1b[:, qc, :],
                                op0=ALU.bypass, op1=ALU.mult,
                                accum_out=colf[:, 1:2])
                        def ln2c():
                            vector.tensor_scalar(
                                colf[:, 10 + qc:11 + qc], colf[:, 0:1],
                                -1.0 / D, None, op0=ALU.mult)
                            vector.tensor_scalar(
                                colf[:, 12 + qc:13 + qc], colf[:, 1:2],
                                1.0 / D, None, op0=ALU.mult)
                            vector.scalar_tensor_tensor(
                                colf[:, 12 + qc:13 + qc],
                                colf[:, 10 + qc:11 + qc],
                                colf[:, 10 + qc:11 + qc],
                                colf[:, 12 + qc:13 + qc],
                                op0=ALU.mult, op1=ALU.subtract)
                            vector.tensor_copy(
                                fdbg[:, fdx(i, 3):fdx(i, 3) + 1],
                                colf[:, 12 + qc:13 + qc]).then_inc(
                                s["a_fdve"], 1)
                        def gqop():
                            vector.wait_ge(s["a_fact"], 2 * qf + 1 + qc)
                            vector.tensor_copy(
                                fdbg[:, fdx(i, 4):fdx(i, 4) + 1],
                                colf[:, 14 + qc:15 + qc])
                            if i >= 2:
                                vector.wait_ge(s["a_gw%d" % qc], 16 * qf)
                            vector.tensor_scalar(
                                gqb[:, qc, :], x1b[:, qc, :],
                                colf[:, 10 + qc:11 + qc],
                                colf[:, 14 + qc:15 + qc],
                                op0=ALU.subtract, op1=ALU.mult).then_inc(
                                s["a_fg"], 1)
                        return t1op, ln1op, x1op, ln2a, ln2b, ln2c, gqop
                    t10, l10, x10, a0, b0, c0, g0 = mk(0)
                    t11, l11, x11, a1, b1, c1, g1 = mk(1)
                    put(0, t10); put(1, t11); put(2, l10)
                    put(4, x10); put(6, a0)
                    put(8, b0); put(10, c0)
                    put(12, l11)
                    put(14, x11)
                    put(16, a1); put(18, b1)
                    put(20, c1)
                    put(22, g0)
                    put(24, g1)
                if qg >= NQG:
                    return out
                def mkpre(qc):
                    def f1():
                        if qg >= 1:
                            vector.wait_ge(s["a_actself"], 4 * (qg - 1) + 2)
                        vector.wait_ge(s["a_x%d%d" % (qg % 2, qc)],
                                       16 * (qg // 2 + 1))
                        vector.reduce_sum(colf[:, 0:1], xq[:, qg % 2, qc, :],
                                          axis=AX.X, negate=True)
                    def f2():
                        vector.scalar_tensor_tensor(
                            sqf[:, :], xq[:, qg % 2, qc, :], 1.0,
                            xq[:, qg % 2, qc, :],
                            op0=ALU.bypass, op1=ALU.mult,
                            accum_out=colf[:, 1:2])
                    def f3():
                        vector.tensor_scalar(colf[:, 4 + qc:5 + qc],
                                             colf[:, 0:1], -1.0 / D,
                                             None, op0=ALU.mult)
                        vector.tensor_scalar(colf[:, 6 + qc:7 + qc],
                                             colf[:, 1:2], 1.0 / D,
                                             None, op0=ALU.mult)
                    def f4():
                        vector.scalar_tensor_tensor(
                            colf[:, 6 + qc:7 + qc], colf[:, 4 + qc:5 + qc],
                            colf[:, 4 + qc:5 + qc], colf[:, 6 + qc:7 + qc],
                            op0=ALU.mult, op1=ALU.subtract)
                        vector.tensor_copy(
                            fdbg[:, fdx(qg * 2 + qc, 1):
                                 fdx(qg * 2 + qc, 1) + 1],
                            colf[:, 6 + qc:7 + qc]).then_inc(s["a_fdve"], 1)
                    return f1, f2, f3, f4
                p10, p20, p30, p40 = mkpre(0)
                p11, p21, p31, p41 = mkpre(1)
                put(26, p10); put(28, p20); put(30, p30); put(32, p40)
                put(34, p11); put(36, p21); put(38, p31); put(40, p41)
                return out

            for qg in range(NQG):
                plan = sched(qg)
                vector.tensor_scalar(
                    omin[:, 0, :], rq_sb[:, :],
                    r_col[:, 0:1], SIM_CAP,
                    op0=ALU.mult, op1=ALU.min)
                for kt in range(NKT):
                    g = glob(qg, kt)
                    p = kt % 2
                    if kt + 1 < NKT:
                        vector.tensor_scalar(
                            omin[:, 1 - p, :], rq_sb[:, :],
                            r_col[:, kt + 1:kt + 2], SIM_CAP,
                            op0=ALU.mult, op1=ALU.min)
                    if g >= 2:
                        vector.wait_ge(s["a_pt"], g - 1)
                    vector.wait_ge(s["a_so"], g + 1)
                    vector.tensor_tensor(
                        simb[:, p, :], ps_so[:, p, 0:256], omin[:, p, :],
                        op=ALU.mult).then_inc(s["a_sim"], 1)
                    for f in plan.pop(kt, []):
                        f()
                for slot in sorted(plan):
                    for f in plan[slot]:
                        f()
                if qg + 1 < NQG:
                    vector.wait_ge(s["a_rq"], qg + 2)
                    vector.tensor_copy(rq_sb[:, :],
                                       ps_so[:, 0, 256:512]).then_inc(
                        s["a_rqc"], 1)
            plan = sched(NQG)
            for slot in sorted(plan):
                for f in plan[slot]:
                    f()

        @block.scalar
        def _(scalar):
            scalar = s["_chains"]["a"].bind(scalar)

            def sched_act(qg):
                out = {}
                def put(slot, f):
                    out.setdefault(slot, []).append(f)
                if qg < NQG:
                    def mk1(qc):
                        def a1():
                            scalar.wait_ge(s["a_fdve"], 4 * qg + 1 + qc)
                            scalar.activation(
                                colf[:, 16:17], colf[:, 6 + qc:7 + qc], AF.Ln,
                                bias=epsb[:, 0:1], scale=-1.0).then_inc(
                                s["a_actself"], 1)
                        def a2():
                            scalar.wait_ge(s["a_actself"], 4 * qg + 1 + qc)
                            scalar.activation(
                                colf[:, 8 + qc:9 + qc], colf[:, 16:17],
                                AF.Exp, scale=-0.5).then_inc(s["a_f1"], 1)
                        return a1, a2
                    a10, a20 = mk1(0)
                    a11, a21 = mk1(1)
                    put(33, a10); put(34, a20)
                    put(41, a11); put(42, a21)
                if qg >= 1:
                    qf = qg - 1
                    def mk2(qc):
                        i = qf * 2 + qc
                        def a3():
                            scalar.wait_ge(s["a_fx1"], i + 1)
                            scalar.dma_start(
                                out=x1_hbm[i * 128:(i + 1) * 128, :],
                                in_=x1b[:, qc, :]).then_inc(
                                s["a_x1w%d" % qc], 16)
                        def a4():
                            scalar.wait_ge(s["a_fdve"], 4 * qf + 3 + qc)
                            scalar.activation(
                                colf[:, 16:17], colf[:, 12 + qc:13 + qc],
                                AF.Ln,
                                bias=epsb[:, 0:1], scale=-1.0).then_inc(
                                s["a_actself"], 1)
                        def a5():
                            scalar.wait_ge(s["a_actself"], 4 * qf + 3 + qc)
                            scalar.activation(
                                colf[:, 14 + qc:15 + qc], colf[:, 16:17],
                                AF.Exp, scale=-0.5).then_inc(s["a_fact"], 1)
                        def a6():
                            scalar.wait_ge(s["a_fg"], i + 1)
                            scalar.dma_start(
                                out=g_hbm[i * 128:(i + 1) * 128, :],
                                in_=gqb[:, qc, :]).then_inc(
                                s["a_gw%d" % qc], 16)
                        return a3, a4, a5, a6
                    a30, a40, a50, a60 = mk2(0)
                    a31, a41, a51, a61 = mk2(1)
                    put(5, a30); put(11, a40); put(12, a50)
                    put(15, a31); put(21, a41); put(22, a51)
                    put(23, a60); put(25, a61)
                return out

            for qg in range(NQG):
                plan = sched_act(qg)
                for kt in range(NKT):
                    g = glob(qg, kt)
                    p = kt % 2
                    scalar.wait_ge(s["a_sim"], g + 1)
                    if g >= 3:
                        scalar.wait_ge(s["a_pv"], g - 2)
                    scalar.activation(ptb[:, g % 3, :], simb[:, p, :],
                                      AF.Exp).then_inc(s["a_pt"], 1)
                    for f in plan.pop(kt, []):
                        f()
                for slot in sorted(plan):
                    for f in plan[slot]:
                        f()
            plan = sched_act(NQG)
            for slot in sorted(plan):
                for f in plan[slot]:
                    f()
            if fdbg_hbm is not None:
                scalar.wait_ge(s["a_fg"], NT)
                scalar.wait_ge(s["a_gw0"], 16 * (NT // 2))
                scalar.wait_ge(s["a_gw1"], 16 * (NT // 2))
                scalar.dma_start(out=fdbg_hbm[:, :], in_=fdbg[:, :]).then_inc(
                    s["a_gw0"], 16)

        @block.tensor
        def _(tensor):
            tensor.wait_ge(s["p_lnT"], 16 * DC)
            tensor.wait_ge(s["a_ones"], 5)
            tensor.wait_ge(s["a_rql0"], 16)
            tensor.matmul(
                ps_so[:, 0, 256:512], ones_a[:, :], r_q[0:1, 0, :],
                start=True, stop=True).then_inc(s["a_rq"], 1)
            for qg in range(NQG):
                q0 = qg * 256
                tensor.wait_ge(s["a_rqc"], qg + 1)
                for kt in range(NKT):
                    g = glob(qg, kt)
                    p = kt % 2
                    k0 = kt * 128
                    if g >= 2:
                        tensor.wait_ge(s["a_sim"], g - 1)
                    ins = None
                    for c in range(DC):
                        ins = tensor.matmul(
                            ps_so[:, p, 0:256],
                            lnT[:, c, k0:k0 + 128],
                            lnT[:, c, q0:q0 + 256],
                            start=(c == 0), stop=(c == DC - 1))
                    ins.then_inc(s["a_so"], 1)
                    if g >= 2:
                        _attn_pv(tensor, s, g - 2, v_sb, ptb, pvs)
                if qg + 1 < NQG:
                    tensor.wait_ge(s["a_sim"], glob(qg, NKT - 1) + 1)
                    tensor.wait_ge(
                        s["a_rql0" if (qg + 1) % 2 == 0 else "a_rql1"],
                        16 * ((qg + 1) // 2 + 1))
                    tensor.matmul(
                        ps_so[:, 0, 256:512],
                        ones_a[:, :],
                        r_q[0:1, (qg + 1) % 2, :],
                        start=True, stop=True).then_inc(s["a_rq"], 1)
            for g in (NQG * NKT - 2, NQG * NKT - 1):
                _attn_pv(tensor, s, g, v_sb, ptb, pvs)


def _attn_pv(tensor, s, g, v_sb, ptb, pvs):
    qg, kt = divmod(g, NKT)
    tensor.wait_ge(s["a_pt"], g + 1)
    tensor.wait_ge(s["a_v%d" % (g % 4)], 16 * (g // 4 + 1))
    if qg >= 1 and kt == 0:
        tensor.wait_ge(s["a_fin"], 2 * qg)
    first = (kt == 0)
    last = (kt == NKT - 1)
    ins = None
    for qc in range(2):
        pv = pvs[qc]
        pt_chunk = ptb[:, g % 3, qc * 128:(qc + 1) * 128]
        for (c0, c1) in ((0, 512), (512, 1024), (1024, 1025)):
            ins = tensor.matmul(
                pv[:, c0:c1],
                pt_chunk,
                v_sb[:, g % 4, c0:c1],
                start=first, stop=last)
    ins.then_inc(s["a_pv"], 1)


def _emit_relayout_router(nc, s, ctx, g_hbm, gt_hbm, rwt_h, rb_h, comb_sb,
                          comb_hbm):
    E = ctx.enter_context
    gt_sb = E(nc.sbuf_tensor("r_gt", [128, DC, T], BF16))
    rwt_f = E(nc.sbuf_tensor("r_rwtf", [128, DC, NE], F32))
    rwt_b = E(nc.sbuf_tensor("r_rwtb", [128, DC, NE], BF16))
    ones_f = E(nc.sbuf_tensor("r_ones", [1, 128], F32))
    rb_f = E(nc.sbuf_tensor("r_rb", [1, NE], F32))
    rbrep = E(nc.sbuf_tensor("r_rbrep", [128, NE], F32))
    lg = E(nc.sbuf_tensor("r_lg", [128, 2, NE], F32))
    sexp = E(nc.sbuf_tensor("r_sexp", [128, 2, NE], F32))
    m8 = E(nc.sbuf_tensor("r_m8", [128, 2, 8], F32))
    mask = E(nc.sbuf_tensor("r_mask", [128, NE], F32))
    ccol = E(nc.sbuf_tensor("r_ccol", [128, 4], F32))
    ps_lg = E(nc.psum_tensor("r_pslg", [128, 2, 512], F32))
    ps_rep = E(nc.psum_tensor("r_psrep", [128, 512], F32))

    with nc.Block("router") as block:

        @block.sync
        def _(sync):
            sync.wait_ge(s["a_gw0"], 16 * (NT // 2 + 1) if comb_hbm is not None
                         else 16 * (NT // 2))
            sync.wait_ge(s["a_gw1"], 16 * (NT // 2))
            sync.wait_ge(s["a_x1w0"], 16 * (NT // 2))
            sync.wait_ge(s["a_x1w1"], 16 * (NT // 2))
            for c in range(DC):
                sync.dma_start_transpose(
                    out=gt_sb[:, c, :],
                    in_=g_hbm[:, c * 128:(c + 1) * 128],
                ).then_inc(s["r_gt"], 16)
            sync.wait_ge(s["r_gt"], 16 * DC)
            for c in range(DC):
                sync.dma_start(
                    out=gt_hbm[c * 128:(c + 1) * 128, :],
                    in_=gt_sb[:, c, :],
                ).then_inc(s["r_gtw"], 16)
            sync.dma_start(out=rwt_f[:, :, :],
                           in_=rwt_h.rearrange("c p e -> p c e")).then_inc(
                s["r_rwt"], 16)
            sync.dma_start(out=rb_f[:, :], in_=rb_h[:, :]).then_inc(
                s["r_rwt"], 16)

        @block.vector
        def _(vector):
            vector = s["_chains"]["d"].bind(vector)
            vector.memset(ones_f[:, :], 1.0).then_inc(s["r_rep"], 1)
            vector.wait_ge(s["r_rwt"], 32)
            vector.tensor_copy(rwt_b[:, :, :], rwt_f[:, :, :]).then_inc(
                s["r_rwtb"], 1)
            vector.wait_ge(s["r_rep"], 2)
            vector.tensor_copy(rbrep[:, :], ps_rep[:, 0:NE])
            for i in range(NT):
                p = i % 2
                vector.wait_ge(s["r_lg"], i + 1)
                if i >= 2:
                    vector.wait_ge(s["r_act"], i - 1)
                vector.tensor_tensor(lg[:, p, :], ps_lg[:, p, 0:NE], rbrep[:, :],
                                     op=ALU.add)
                vector.tensor_copy(ccol[:, 3:4], rbrep[:, 0:1])
                vector.reduce_max(ccol[:, 0:1], lg[:, p, :], axis=AX.X,
                                  negate=True)
                vector.tensor_copy(ccol[:, 3:4],
                                   ccol[:, 0:1]).then_inc(s["r_dve"], 1)
                vector.wait_ge(s["r_act"], i + 1)
                vector.tensor_copy(ccol[:, 3:4], rbrep[:, 0:1])
                vector.tensor_copy(ccol[:, 3:4], rbrep[:, 1:2])
                vector.max(out=m8[:, p, :], in_=sexp[:, p, :])
                vector.tensor_copy(ccol[:, 3:4], rbrep[:, 0:1])
                vector.reduce_sum(ccol[:, 1:2], m8[:, p, 0:2], axis=AX.X)
                vector.tensor_scalar(mask[:, :], sexp[:, p, :],
                                     m8[:, p, 1:2], None, op0=ALU.is_ge)
                vector.reciprocal(ccol[:, 2:3], ccol[:, 1:2])
                vector.tensor_copy(ccol[:, 3:4], rbrep[:, 0:1])
                vector.tensor_copy(ccol[:, 3:4], rbrep[:, 1:2])
                vector.scalar_tensor_tensor(
                    comb_sb[:, i * NE:(i + 1) * NE], sexp[:, p, :],
                    ccol[:, 2:3], mask[:, :],
                    op0=ALU.mult, op1=ALU.mult).then_inc(s["r_comb"], 1)

        @block.scalar
        def _(scalar):
            scalar = s["_chains"]["a"].bind(scalar)
            for i in range(NT):
                p = i % 2
                scalar.wait_ge(s["r_dve"], i + 1)
                if i >= 2:
                    scalar.wait_ge(s["r_comb"], i - 1)
                scalar.activation(sexp[:, p, :], lg[:, p, :], AF.Exp,
                                  bias=ccol[:, 0:1]).then_inc(s["r_act"], 1)
            if comb_hbm is not None:
                scalar.wait_ge(s["r_comb"], NT)
                scalar.dma_start(out=comb_hbm[:, :], in_=comb_sb[:, :]).then_inc(
                    s["r_combw"], 16)
                scalar.wait_ge(s["r_combw"], 16)

        @block.tensor
        def _(tensor):
            tensor.wait_ge(s["r_rwt"], 32)
            tensor.wait_ge(s["r_rep"], 1)
            tensor.matmul(ps_rep[:, 0:NE], ones_f[:, :], rb_f[:, :],
                          start=True, stop=True).then_inc(s["r_rep"], 1)
            tensor.wait_ge(s["r_gt"], 16 * DC)
            tensor.wait_ge(s["r_rwtb"], 1)
            for i in range(NT):
                p = i % 2
                if i >= 2:
                    tensor.wait_ge(s["r_dve"], i - 1)
                ins = None
                for c in range(DC):
                    ins = tensor.matmul(
                        ps_lg[:, p, 0:NE],
                        gt_sb[:, c, i * 128:(i + 1) * 128],
                        rwt_b[:, c, :],
                        start=(c == 0), stop=(c == DC - 1))
                ins.then_inc(s["r_lg"], 1)


def _emit_ffn(nc, s, ctx, gt_hbm, x1_hbm, acc_hbm, out_h, comb_sb,
              w1t_sh_h, w2t_sh_h, b1_sh_h, b2_sh_h,
              w1t_sp_h, w2t_sp_h, b1_sp_h, b2_sp_h):
    E = ctx.enter_context
    w1 = E(nc.sbuf_tensor("f_w1", [128, DC, DFF], BF16))
    w2 = E(nc.sbuf_tensor("f_w2", [128, FC, D], BF16))
    h_sb = E(nc.sbuf_tensor("f_h", [128, FC, 512], BF16))
    g_sb = E(nc.sbuf_tensor("f_g", [128, DC, 512], BF16))
    acc = E(nc.sbuf_tensor("f_acc", [128, 1, D], F32))
    xs = E(nc.sbuf_tensor("f_xs", [128, 3, 1024], F32))
    ub = E(nc.sbuf_tensor("f_ub", [128, 1024], F32))
    tb = E(nc.sbuf_tensor("f_tb", [128, 2, 1024], F32))
    sg = E(nc.sbuf_tensor("f_sg", [128, 2, 1024], F32))
    b1c = E(nc.sbuf_tensor("f_b1", [128, FC], F32))
    b2b = E(nc.sbuf_tensor("f_b2b", [33, D // 2], BF16))
    onesb = E(nc.sbuf_tensor("f_ones", [33, 128], BF16))
    ph = E(nc.psum_tensor("f_ph", [128, 4, 512], F32))
    py0 = E(nc.psum_tensor("f_py0", [128, D], F32))
    py1 = E(nc.psum_tensor("f_py1", [128, D], F32))
    pys = [py0, py1]

    def fcg(p_, tg, fc):
        return (p_ * NTG + tg) * FC + fc

    def tcg(p_, tg, tc):
        return (p_ * NTG + tg) * 4 + tc

    def tgg(p_, tg):
        return p_ * NTG + tg

    NPA = _NPASS
    NQ_ = FC // 2
    LVL = {"dma": 0, "mm1": 1, "gelu": 2, "mm2": 3, "full": 4}[_FFN_LEVEL]

    with nc.Block("ffn") as block:

        @block.gpsimd
        def _(gpsimd):
            gpsimd = s["_chains"]["g"].bind(gpsimd)
            gpsimd.wait_ge(s["r_gtw"], 16 * DC)
            gpsimd.wait_ge(s["a_x1w0"], 16 * (NT // 2))
            gpsimd.wait_ge(s["a_x1w1"], 16 * (NT // 2))
            gpsimd.wait_ge(s["a_gw0"], 16 * (NT // 2))
            gpsimd.wait_ge(s["a_gw1"], 16 * (NT // 2))
            for p_ in range(NPA):
                w1_src = w1t_sh_h if p_ == 0 else w1t_sp_h[p_ - 1]
                w2_src = w2t_sh_h if p_ == 0 else w2t_sp_h[p_ - 1]
                b1_src = b1_sh_h if p_ == 0 else b1_sp_h[p_ - 1]
                b2_src = b2_sh_h if p_ == 0 else b2_sp_h[p_ - 1]
                if p_ > 0 and LVL >= 1:
                    gpsimd.wait_ge(s["f_mm1"], (NTG * p_ - 1) * FC + 16)
                for hh in range(2):
                    gpsimd.dma_start(
                        out=w1[:, hh * 4:(hh + 1) * 4, 0:2048],
                        in_=w1_src[hh * 512:(hh + 1) * 512, 0:2048].rearrange(
                            "(c p) f -> p c f", p=128),
                    ).then_inc(s["f_w1a"], 16)
                if p_ > 0 and LVL >= 1:
                    gpsimd.wait_ge(s["f_mm1"], NTG * p_ * FC)
                for hh in range(2):
                    gpsimd.dma_start(
                        out=w1[:, hh * 4:(hh + 1) * 4, 2048:4096],
                        in_=w1_src[hh * 512:(hh + 1) * 512, 2048:4096].rearrange(
                            "(c p) f -> p c f", p=128),
                    ).then_inc(s["f_w1b"], 16)
                if p_ > 0 and LVL >= 2:
                    gpsimd.wait_ge(s["f_xse"], NTG * NQ_ * p_)
                    gpsimd.wait_ge(s["f_xso"], NTG * NQ_ * p_)
                gpsimd.dma_start(out=b1c[:, :], in_=b1_src[:, :]).then_inc(
                    s["f_bias1"], 16)
                if p_ > 0 and LVL >= 3:
                    gpsimd.wait_ge(s["f_mm2"], 4 * NTG * p_)
                gpsimd.dma_start(out=b2b[0:1, :],
                                 in_=b2_src[0:1, :]).then_inc(s["f_bias2"], 16)
                gpsimd.dma_start(out=b2b[32:33, :],
                                 in_=b2_src[1:2, :]).then_inc(s["f_bias2"], 16)
                for hh in range(4):
                    gpsimd.dma_start(
                        out=w2[:, hh * 4:(hh + 1) * 4, :],
                        in_=w2_src[hh * 512:(hh + 1) * 512, :].rearrange(
                            "(c p) d -> p c d", p=128),
                    ).then_inc(s["f_w2a"], 16)
                for hh in range(4):
                    gpsimd.dma_start(
                        out=w2[:, 16 + hh * 4:16 + (hh + 1) * 4, :],
                        in_=w2_src[2048 + hh * 512:2048 + (hh + 1) * 512, :].rearrange(
                            "(c p) d -> p c d", p=128),
                    ).then_inc(s["f_w2b"], 16)

        @block.sync
        def _(sync):
            sync.wait_ge(s["r_gtw"], 16 * DC)
            sync.wait_ge(s["a_x1w0"], 16 * (NT // 2))
            sync.wait_ge(s["a_x1w1"], 16 * (NT // 2))
            for p_ in range(NPA):
                for tg in range(NTG):
                    k0 = tgg(p_, tg)
                    if LVL >= 1:
                        if k0 >= 1:
                            sync.wait_ge(s["f_mm1"], FC * k0)
                        sync.dma_start(
                            out=g_sb[:, :, :],
                            in_=gt_hbm[:, tg * 512:(tg + 1) * 512].rearrange(
                                "(c p) t -> p c t", p=128),
                        ).then_inc(s["f_g"], 16)
                    if LVL >= 4:
                        srcx = x1_hbm if p_ == 0 else acc_hbm
                        for P in range(4):
                            sync.wait_ge(s["f_accst"], 16 * (4 * k0 + P))
                            r0 = tg * 512 + P * 128
                            sync.dma_start(
                                out=acc[:, 0, :],
                                in_=srcx[r0:r0 + 128, :],
                            ).then_inc(s["f_acc"], 16)
            if LVL >= 4:
                sync.wait_ge(s["f_accst"], 64 * NPA * NTG)
            elif LVL >= 2:
                if _POOL_OFF:
                    sync.wait_ge(s["f_sg"], NPA * NTG * FC)
                else:
                    sync.wait_ge(s["f_h"], NPA * NTG * FC)
            elif LVL >= 1:
                sync.wait_ge(s["f_mm1"], FC * NPA * NTG)
            else:
                sync.wait_ge(s["f_w2b"], 16 * NPA)

        @block.vector
        def _(vector):
            vector = s["_chains"]["d"].bind(vector)
            vector.memset(onesb[:, :], 1.0).then_inc(s["f_b2r"], 1)
            for p_ in range(NPA):
                if LVL < 2:
                    continue
                vector.wait_ge(s["f_bias1"], 16 * (p_ + 1))
                vector.engine_nop().then_inc(s["f_b2r"], 1)
                for tg in range(NTG):
                    if LVL >= 2:
                        NQ = FC // 2
                        for q in range(NQ):
                            kq = (p_ * NTG + tg) * NQ + q
                            fc = 2 * q
                            k = fcg(p_, tg, fc)
                            vector.wait_ge(s["f_mm1"], k + 1)
                            if kq >= 3:
                                vector.wait_ge(s["f_u"], kq - 2)
                            vector.tensor_scalar(
                                xs[:, kq % 3, 0:512], ph[:, fc % 4, :],
                                b1c[:, fc:fc + 1], None,
                                op0=ALU.add).then_inc(s["f_xse"], 1)
                            if q >= 2:
                                vector.wait_ge(s["f_sg"], kq - 1)
                                vector.tensor_tensor(
                                    h_sb[:, 2 * (q - 2):2 * (q - 1), :],
                                    xs[:, (kq - 2) % 3, :],
                                    sg[:, (q - 2) % 2, :],
                                    op=ALU.mult).then_inc(s["f_h"], 1)
                            vector.wait_ge(s["f_u"], kq + 1)
                            if kq >= 2:
                                vector.wait_ge(s["f_sg"], kq - 1)
                            vector.scalar_tensor_tensor(
                                tb[:, kq % 2, :], ub[:, :], 1.0,
                                xs[:, kq % 3, :],
                                op0=ALU.add, op1=ALU.mult).then_inc(s["f_t"], 1)
                        for qh in (NQ - 2, NQ - 1):
                            kq = (p_ * NTG + tg) * NQ + qh
                            vector.wait_ge(s["f_sg"], kq + 1)
                            vector.tensor_tensor(
                                h_sb[:, 2 * qh:2 * qh + 2, :],
                                xs[:, kq % 3, :],
                                sg[:, qh % 2, :],
                                op=ALU.mult).then_inc(s["f_h"], 1)
                    if LVL >= 4:
                        for tc in range(4):
                            kt = tcg(p_, tg, tc)
                            vector.wait_ge(s["f_acc"],
                                           16 * (4 * tgg(p_, tg) + tc + 1))
                            vector.wait_ge(s["f_mm2"], kt + 1)
                            if p_ == 0:
                                we = 1.0
                            else:
                                itile = tg * 4 + tc
                                we = comb_sb[:, itile * NE + (p_ - 1):
                                             itile * NE + p_]
                            vector.scalar_tensor_tensor(
                                acc[:, 0, :], pys[tc % 2][:, :], we,
                                acc[:, 0, :],
                                op0=ALU.mult, op1=ALU.add).then_inc(s["f_accd"], 1)

        @block.scalar
        def _(scalar):
            scalar = s["_chains"]["a"].bind(scalar)
            for p_ in range(NPA):
                if LVL < 2:
                    continue
                for tg in range(NTG):
                    if LVL >= 4:
                        tp_, tgp = (p_, tg - 1) if tg > 0 else (p_ - 1, NTG - 1)
                        if tp_ >= 0:
                            kprev = tgg(tp_, tgp)
                            dstp = out_h if tp_ == NPA - 1 else acc_hbm
                            for P in range(4):
                                scalar.wait_ge(s["f_accd"], 4 * kprev + P + 1)
                                r0 = tgp * 512 + P * 128
                                scalar.dma_start(
                                    out=dstp[r0:r0 + 128, :],
                                    in_=acc[:, 0, :],
                                ).then_inc(s["f_accst"], 16)
                    NQ = FC // 2
                    if tg == 0:
                        scalar.wait_ge(s["f_bias1"], 16 * (p_ + 1))
                    for q in range(NQ):
                        fc = 2 * q + 1
                        k = fcg(p_, tg, fc)
                        kq = (p_ * NTG + tg) * NQ + q
                        scalar.wait_ge(s["f_mm1"], k + 1)
                        if kq >= 3:
                            scalar.wait_ge(s["f_t"], kq - 2)
                            scalar.wait_ge(s["f_h"], kq - 2)
                        scalar.activation(
                            xs[:, kq % 3, 512:1024],
                            ph[:, fc % 4, :], AF.Identity,
                            bias=b1c[:, fc:fc + 1]).then_inc(s["f_xso"], 1)
                        scalar.wait_ge(s["f_xse"], kq + 1)
                        if kq >= 1:
                            scalar.wait_ge(s["f_t"], kq)
                        scalar.activation(
                            ub[:, :], xs[:, kq % 3, :], AF.Square,
                            scale=GELU_SQA).then_inc(s["f_u"], 1)
                        if q >= 1:
                            scalar.wait_ge(s["f_t"], kq)
                            if kq >= 2:
                                scalar.wait_ge(s["f_h"], kq - 2)
                            scalar.activation(
                                sg[:, (q - 1) % 2, :],
                                tb[:, (kq - 1) % 2, :], AF.Sigmoid,
                                scale=GELU_2C).then_inc(s["f_sg"], 1)
                    if LVL >= 2:
                        kq = (p_ * NTG + tg) * (FC // 2) + FC // 2 - 1
                        scalar.wait_ge(s["f_t"], kq + 1)
                        if kq >= 2:
                            scalar.wait_ge(s["f_h"], kq - 1)
                        scalar.activation(
                            sg[:, (FC // 2 - 1) % 2, :], tb[:, kq % 2, :],
                            AF.Sigmoid,
                            scale=GELU_2C).then_inc(s["f_sg"], 1)
                if LVL >= 4 and p_ == NPA - 1 and tg == NTG - 1:
                    kprev = tgg(p_, tg)
                    for P in range(4):
                        scalar.wait_ge(s["f_accd"], 4 * kprev + P + 1)
                        r0 = tg * 512 + P * 128
                        scalar.dma_start(
                            out=out_h[r0:r0 + 128, :],
                            in_=acc[:, 0, :],
                        ).then_inc(s["f_accst"], 16)

        @block.tensor
        def _(tensor):
            for p_ in range(NPA):
                if LVL < 1:
                    continue
                for tg in range(NTG):
                    k0 = tgg(p_, tg)
                    for fc in range(FC):
                        pp = fc % 4
                        k = fcg(p_, tg, fc)
                        w1sem = s["f_w1a"] if fc < 16 else s["f_w1b"]
                        tensor.wait_ge(w1sem, 32 * (p_ + 1))
                        tensor.wait_ge(s["f_g"], 16 * (k0 + 1))
                        if k >= 4 and LVL >= 2:
                            kf = k - 4
                            if kf % 2 == 0:
                                tensor.wait_ge(s["f_xse"], kf // 2 + 1)
                            else:
                                tensor.wait_ge(s["f_xso"], kf // 2 + 1)
                        ins = None
                        for c in range(DC):
                            ins = tensor.matmul(
                                ph[:, pp, :],
                                w1[:, c, fc * 128:(fc + 1) * 128],
                                g_sb[:, c, :],
                                start=(c == 0), stop=(c == DC - 1))
                        ins.then_inc(s["f_mm1"], 1)

                    if LVL < 3:
                        continue
                    tensor.wait_ge(s["f_b2r"], 1)
                    tensor.wait_ge(s["f_bias2"], 32 * (p_ + 1))
                    for tc in range(4):
                        kt = tcg(p_, tg, tc)
                        py = pys[tc % 2]
                        if kt >= 2 and LVL >= 4:
                            tensor.wait_ge(s["f_accd"], kt - 1)
                        for fc in range(FC):
                            w2sem = s["f_w2a"] if fc < 16 else s["f_w2b"]
                            tensor.wait_ge(w2sem, 64 * (p_ + 1))
                            if tc == 0:
                                tensor.wait_ge(
                                    s["f_h"],
                                    (p_ * NTG + tg) * (FC // 2) + fc // 2 + 1)
                            for dmh in range(2):
                                tensor.matmul(
                                    py[:, dmh * 512:(dmh + 1) * 512],
                                    h_sb[:, fc, tc * 128:(tc + 1) * 128],
                                    w2[:, fc, dmh * 512:(dmh + 1) * 512],
                                    start=(fc == 0), stop=False)
                        ins = None
                        for dmh in range(2):
                            ins = tensor.matmul(
                                py[:, dmh * 512:(dmh + 1) * 512],
                                onesb[32 * dmh:32 * dmh + 1, :],
                                b2b[32 * dmh:32 * dmh + 1, :],
                                start=False, stop=True)
                        ins.then_inc(s["f_mm2"], 1)



def kernel(L0_attn_x, L0_ffn_shared_w1, L0_ffn_shared_b1, L0_ffn_shared_w2,
           L0_ffn_shared_b2, L0_ffn_router_w, L0_ffn_router_b,
           L0_ffn_spec_w1, L0_ffn_spec_b1, L0_ffn_spec_w2, L0_ffn_spec_b2):
    x = np.ascontiguousarray(np.asarray(L0_attn_x, dtype=np.float32))
    B = x.shape[0]

    def c(a):
        return np.ascontiguousarray(np.asarray(a, dtype=np.float32))

    w1t_sh = c(np.asarray(L0_ffn_shared_w1).T)
    w2t_sh = c(np.asarray(L0_ffn_shared_w2).T)
    b1_sh = c(np.asarray(L0_ffn_shared_b1).reshape(FC, 128).T)
    b2_sh = c(np.asarray(L0_ffn_shared_b2).reshape(2, D // 2))
    w1t_sp = c(np.transpose(np.asarray(L0_ffn_spec_w1), (0, 2, 1)))
    w2t_sp = c(np.transpose(np.asarray(L0_ffn_spec_w2), (0, 2, 1)))
    b1_sp = c(np.transpose(np.asarray(L0_ffn_spec_b1).reshape(NE, FC, 128),
                           (0, 2, 1)))
    b2_sp = c(np.asarray(L0_ffn_spec_b2).reshape(NE, 2, D // 2))
    rwt = c(np.asarray(L0_ffn_router_w).T.reshape(DC, 128, NE))
    rb = c(np.asarray(L0_ffn_router_b).reshape(1, NE))

    nc = build_nc()
    in_maps = []
    for b in range(B):
        in_maps.append({
            "x": np.ascontiguousarray(x[b]),
            "w1t_sh": w1t_sh, "w2t_sh": w2t_sh, "b1_sh": b1_sh, "b2_sh": b2_sh,
            "w1t_sp": w1t_sp, "w2t_sp": w2t_sp, "b1_sp": b1_sp, "b2_sp": b2_sp,
            "rwt": rwt, "rb": rb,
        })
    res = run_bass_kernel_spmd(nc, in_maps, core_ids=list(range(B)))
    out = np.stack([res.results[b]["out"] for b in range(B)], axis=0)
    return out
